# version 14
# baseline (speedup 1.0000x reference)
import numpy as np
import ml_dtypes
from contextlib import ExitStack

import concourse.bass as bass
import concourse.mybir as mybir
from concourse.alu_op_type import AluOpType as OP
from concourse.bass_utils import run_bass_kernel_spmd

DEBUG_RACES = False

F32 = mybir.dt.float32
BF16 = mybir.dt.bfloat16
AF = mybir.ActivationFunctionType
AXX = mybir.AxisListType.X
BF = ml_dtypes.bfloat16

D = 1024
T = 4096
BATCH = 4
H = 16
DH = 64
LAT = 256
E = 8
ISH = 2048
NT = 2048
TT = 512
NTT = NT // TT
DC = D // 128
LC = LAT // 128
HL = H // 2
EPS = 1e-5
NQG = T // TT
NKT = T // 128
NEX = E + 2

RG = [[0, 1], [2, 3], [4, 5], [6, 7]]

AXT = 0
AH = 16384
ALN = 16384
AW1 = 32768
AW3 = 40960
AW2 = 49152
ARENA = 53248
ALAT = 0
AQ = 8192
AK = 16384
AV = 24576
APT = 32896
AO = 34944
AWS = 43136
AXTI = 47232
ASP = 51328


class Pass:

    def __init__(self, which, eng, semh):
        self.which = which
        self.eng = eng
        self.semh = semh
        self.c = {k: 0 for k in semh}

    def op(self, engname, fname, *args, sem=None, inc=1, **kw):
        if sem is not None:
            self.c[sem] += inc
        if self.which == engname:
            inst = getattr(self.eng, fname)(*args, **kw)
            if sem is not None:
                inst.then_inc(self.semh[sem], inc)
        return self.c[sem] if sem is not None else None

    def dma(self, engname, out, in_, sem):
        return self.op(engname, "dma_start", sem=sem, inc=16, out=out, in_=in_)

    def wait(self, engname, sem, val):
        if val is None or val <= 0:
            return
        if self.which == engname:
            self.eng.wait_ge(self.semh[sem], val)

    def waitc(self, engname, sem):
        self.wait(engname, sem, self.c[sem])

    def drain(self, engname):
        if self.which == engname:
            self.eng.drain()

    def snap(self, sem):
        return self.c[sem]


def build_program(stage="full"):
    nc = bass.Bass("TRN2", target_bir_lowering=False,
                   detect_race_conditions=DEBUG_RACES)
    dr = {}

    def din(name, shape, dt=F32):
        dr[name] = nc.dram_tensor(name, list(shape), dt, kind="ExternalInput")

    din("x_sh", [D, NT])
    din("w1t", [E, D, D])
    din("w3t", [E, D, D])
    din("w2t", [E, D, D])
    din("sw1t", [D, ISH])
    din("sw3t", [D, ISH])
    din("sw2t", [ISH, D])
    din("gwt", [D, E])
    din("gbias", [1, E])
    din("wq_sh", [D, HL * DH])
    din("wdkv", [D, LAT])
    din("wuk_sh", [LAT, HL * DH])
    din("wuv_sh", [LAT, HL * DH])
    din("wo_sh", [HL * DH, D])
    din("ln_g", [128, LC])
    din("ln_b", [128, LC])
    din("ident", [128, 128])
    din("ones_bf", [128, 128], BF16)
    din("onesrow_f", [1, 128])
    din("headones", [128, 8], BF16)
    din("cos2", [128, T], BF16)
    din("sin2", [128, T], BF16)
    din("masks", [128, 4 * TT], BF16)
    din("ccol", [128, 4])
    din("sel8", [8, E * 128], BF16)
    din("sel2", [33, 128])

    out_t = nc.dram_tensor("out", [D, NT], F32, kind="ExternalOutput")
    if stage in ("ffn1",):
        g_dbg = nc.dram_tensor("dbg_gates", [8, NT], BF16, kind="ExternalOutput")
        x_dbg = nc.dram_tensor("dbg_xt", [128, 2048], BF16, kind="ExternalOutput")
        h_dbg = nc.dram_tensor("dbg_h", [128, 2048], BF16, kind="ExternalOutput")

    g0 = dict(dr=dr, out=out_t, stage=stage)
    if stage in ("ffn1",):
        g0["dbg_gates"] = g_dbg
        g0["dbg_xt"] = x_dbg
        g0["dbg_h"] = h_dbg
    g0["w1s"] = nc.dram_tensor("w1s", [NEX, 128, 8192], BF16)
    g0["w3s"] = nc.dram_tensor("w3s", [NEX, 128, 8192], BF16)
    g0["w2s"] = nc.dram_tensor("w2s", [NEX, 128, 8192], BF16)
    g0["xt_bounce"] = nc.dram_tensor("xt_bounce", [D, NT], BF16)
    g0["xt_gath"] = nc.dram_tensor("xt_gath", [2 * D, NT], BF16)
    g0["yacc"] = nc.dram_tensor("yacc", [D, T], BF16)
    g0["y_bounce"] = nc.dram_tensor("y_bounce", [2 * D, NT], BF16)
    g0["y_red"] = nc.dram_tensor("y_red", [D, NT], BF16)

    es = ExitStack()
    with es:
        def sb(name, shape, dt):
            g0[name] = es.enter_context(nc.sbuf_tensor(name, list(shape), dt))

        sb("x_res", [128, DC * NT], F32)
        sb("A", [128, ARENA], BF16)
        sb("scr", [128, 2048], F32)
        sb("smf", [128, 1024], F32)
        sb("identS", [128, 128], F32)
        sb("ones_bfS", [128, 128], BF16)
        sb("onesrow_fS", [1, 128], F32)
        sb("headonesS", [128, 8], BF16)
        sb("cos2S", [128, T], BF16)
        sb("sin2S", [128, T], BF16)
        sb("masksS", [128, 4 * TT], BF16)
        sb("ccolS", [128, 4], F32)
        sb("gwt_sb", [128, DC * E], F32)
        sb("gb_sb", [1, E], F32)
        sb("lng_sb", [128, LC], F32)
        sb("lnb_sb", [128, LC], F32)
        sb("gates_fm", [8, NT], BF16)
        sb("sel8S", [8, E * 128], BF16)
        sb("small2", [128, 48], F32)

        for nm, shape in [("pbig0", [128, 1024]), ("pbig1", [128, 1024]),
                          ("pbig2", [128, 1024]),
                          ("p6", [128, 512]), ("p7", [128, 512])]:
            g0[nm] = es.enter_context(nc.psum_tensor(nm, shape, F32))
        g0["p4"] = g0["pbig2"][:, 0:512]
        g0["p5"] = g0["pbig2"][:, 512:1024]

        sems = ["cst", "xld", "sw1", "sw3", "sw2", "tw1", "tw3", "tw2",
                "xta", "xtb", "ypa", "ypb", "yra", "yrb", "bnc", "bna",
                "bnb", "gw1", "gw3", "gw2", "gwa", "scd", "outw", "sl2",
                "mm", "vv", "ss", "cc", "gg"]
        semh = {n: es.enter_context(nc.semaphore(n)) for n in sems}
        blk = es.enter_context(nc.Block())

        def run(which):
            def f(e):
                program(Pass(which, e, semh), g0)
            return f

        blk.sync(run("sync"))
        blk.tensor(run("tensor"))
        blk.vector(run("vector"))
        blk.scalar(run("scalar"))
        blk.gpsimd(run("gpsimd"))

    return nc



def xr(g, c, t0, t1):
    return g["x_res"][:, c * NT + t0: c * NT + t1]


def av(g, off, n):
    return g["A"][:, off: off + n]


def avr(g, off, n, p0, p1):
    return g["A"][p0:p1, off: off + n]


def program(P, g0):
    g = dict(g0)
    g["_st"] = {}
    stage = g["stage"]

    emit_const_loads(P, g)
    emit_xload(P, g)

    emit_ffn_sublayer(P, g, app=0)
    if stage == "ffn1":
        emit_output(P, g)
        return
    emit_attention(P, g)
    if stage == "attn":
        emit_output(P, g)
        return
    nex2 = NEX
    if stage.startswith("full") and len(stage) > 4:
        nex2 = int(stage[4:])
    emit_ffn_sublayer(P, g, app=1, nex=nex2)
    emit_output(P, g)


def emit_const_loads(P, g):
    dr = g["dr"]
    for s, d in [("identS", "ident"), ("ones_bfS", "ones_bf"),
                 ("onesrow_fS", "onesrow_f"), ("headonesS", "headones"),
                 ("cos2S", "cos2"), ("sin2S", "sin2"), ("masksS", "masks"),
                 ("ccolS", "ccol"), ("gb_sb", "gbias"), ("lng_sb", "ln_g"),
                 ("lnb_sb", "ln_b"), ("sel8S", "sel8")]:
        P.dma("sync", g[s][:, :], dr[d][:, :], "cst")
    src = bass.AP(dr["gwt"], 0, [[E, 128], [128 * E, DC], [1, E]])
    P.dma("sync", g["gwt_sb"][:, :].rearrange("p (c e) -> p c e", c=DC), src, "cst")
    g["const_sld"] = P.snap("cst")


def emit_xload(P, g):
    src = bass.AP(g["dr"]["x_sh"], 0, [[NT, 128], [128 * NT, DC], [1, NT]])
    P.dma("sync", g["x_res"][:, :].rearrange("p (c t) -> p c t", c=DC), src, "xld")
    g["xload_sld"] = P.snap("xld")



def emit_ln(P, g, with_gate, bounce=None):
    A, scr, smf = g["A"], g["scr"], g["smf"]
    S_ps, Q_ps = g["p4"], g["p5"]
    st = g["_st"]
    P.wait("vector", "mm", P.snap("mm"))
    P.wait("scalar", "mm", P.snap("mm"))
    P.wait("vector", "gg", P.snap("gg"))
    P.wait("scalar", "gg", P.snap("gg"))
    P.wait("vector", "xld", P.snap("xld"))
    P.drain("vector")
    P.drain("scalar")
    ln_mm = {}
    for nt in range(NTT):
        for c in range(DC):
            xb = av(g, ALN + (c % 2) * 512, 512)
            xq = av(g, ALN + 1024 + (c % 2) * 512, 512)
            if c >= 2:
                P.wait("vector", "mm", ln_mm[c - 2])
            P.drain("vector")
            P.op("vector", "tensor_copy", sem="vv", out=xb,
                 in_=xr(g, c, nt * TT, (nt + 1) * TT))
            P.waitc("scalar", "vv")
            P.drain("scalar")
            P.op("scalar", "activation", sem="ss", out=xq, in_=xb, func=AF.Square)
            P.waitc("tensor", "vv")
            P.waitc("tensor", "ss")
            P.wait("tensor", "cst", g["const_sld"])
            P.op("tensor", "matmul", S_ps[:, :], g["ones_bfS"][:, :], xb,
                 start=(c == 0), stop=(c == DC - 1))
            ln_mm[c] = P.op("tensor", "matmul", Q_ps[:, :], g["ones_bfS"][:, :],
                            xq, start=(c == 0), stop=(c == DC - 1), sem="mm")
        stats_mm = P.snap("mm")
        mu2, var, xm, rstd = (scr[:, 0:512], scr[:, 512:1024],
                              scr[:, 1024:1536], scr[:, 1536:2048])
        P.wait("scalar", "mm", stats_mm)
        P.drain("scalar")
        P.op("scalar", "activation", sem="ss", out=mu2, in_=S_ps[:, :],
             func=AF.Square, scale=1.0 / D)
        P.waitc("vector", "ss")
        P.wait("vector", "mm", stats_mm)
        P.drain("vector")
        P.op("vector", "scalar_tensor_tensor", sem="vv", out=var, in0=Q_ps[:, :],
             scalar=1.0 / D, in1=mu2, op0=OP.mult, op1=OP.subtract)
        P.waitc("scalar", "vv")
        P.drain("scalar")
        P.op("scalar", "activation", sem="ss", out=rstd, in_=var, func=AF.Ln,
             bias=g["ccolS"][:, 0:1], scale=1.0)
        P.drain("scalar")
        P.op("scalar", "activation", sem="ss", out=rstd, in_=rstd, func=AF.Exp,
             scale=-0.5)
        recip_vv = P.snap("vv")
        rstd_ss = P.snap("ss")
        P.wait("vector", "ss", rstd_ss)
        for c in range(DC):
            P.drain("vector")
            P.op("vector", "scalar_tensor_tensor", out=xm, in0=S_ps[:, :],
                 scalar=-1.0 / D, in1=xr(g, c, nt * TT, (nt + 1) * TT),
                 op0=OP.mult, op1=OP.add)
            P.drain("vector")
            xtv = av(g, AXT + c * NT + nt * TT, TT)
            xt_vv = P.op("vector", "tensor_tensor", sem="vv", out=xtv, in0=xm,
                         in1=rstd, op=OP.mult)
            if with_gate:
                x32 = smf[:, (c % 2) * 512:(c % 2) * 512 + 512]
                if c >= 2:
                    P.wait("vector", "mm", st.get(("gmm", c - 2), 0))
                P.drain("vector")
                P.op("vector", "tensor_tensor", sem="vv", out=x32, in0=xm,
                     in1=rstd, op=OP.mult)
                P.waitc("tensor", "vv")
                gv = P.op("tensor", "matmul", g["p6"][0:E, :],
                          g["gwt_sb"][:, c * E:(c + 1) * E], x32,
                          start=(c == 0), stop=(c == DC - 1), sem="mm")
                st[("gmm", c)] = gv
            if bounce is not None:
                eng = "sync" if c % 2 == 0 else "scalar"
                q = "bnc" if c % 2 == 0 else "scd"
                P.wait(eng, "vv", xt_vv)
                dst = bass.AP(bounce, c * 128 * NT + nt * TT, [[NT, 128], [1, TT]])
                P.dma(eng, dst, xtv, q)
        if with_gate:
            emit_gate_math(P, g, nt)
    g["xt_vv"] = P.snap("vv")


def emit_gate_math(P, g, nt):
    scr, smf, small2 = g["scr"], g["smf"], g["small2"]
    G_ps, T_ps = g["p6"], g["p7"]
    st = g["_st"]
    gate_mm = P.snap("mm")
    gsc = scr[0:E, 0:512]
    P.wait("vector", "mm", gate_mm)
    P.drain("vector")
    P.op("vector", "tensor_copy", sem="vv", out=gsc, in_=G_ps[0:E, :])
    gevac = P.snap("vv")

    if not st.get("bias8"):
        st["bias8"] = True
        P.wait("tensor", "cst", g["const_sld"])
        b8v = P.op("tensor", "matmul", T_ps[:, 0:E], g["onesrow_fS"][:, :],
                   g["gb_sb"][:, :], start=True, stop=True, sem="mm")
        P.wait("vector", "mm", b8v)
        P.drain("vector")
        P.op("vector", "tensor_copy", sem="vv", out=small2[:, 32:40],
             in_=T_ps[:, 0:E])
        st["bias8_vv"] = P.snap("vv")
    bias8 = small2[:, 32:40]

    for sub in range(4):
        gt = T_ps[:, 8 + sub * 8: 16 + sub * 8]
        P.wait("tensor", "vv", gevac)
        trv = P.op("tensor", "transpose", gt, gsc[:, sub * 128:(sub + 1) * 128],
                   g["identS"][0:E, 0:E], sem="mm")
        nmax, sum8, rec = (smf[:, 0:1], smf[:, 1:2], smf[:, 2:3])
        e8, b8, m8, msk, gtm = (small2[:, 0:8], small2[:, 8:16],
                                small2[:, 16:24], small2[:, 24:32],
                                small2[:, 40:48])
        P.wait("vector", "mm", trv)
        P.drain("vector")
        P.op("vector", "reduce_max", sem="vv", out=nmax, in_=gt, axis=AXX,
             negate=True)
        P.waitc("scalar", "vv")
        P.drain("scalar")
        P.op("scalar", "activation", sem="ss", out=e8, in_=gt, func=AF.Exp,
             bias=nmax, accum_out=sum8)
        P.waitc("vector", "ss")
        P.drain("vector")
        P.op("vector", "reciprocal", out=rec, in_=sum8)
        P.drain("vector")
        P.op("vector", "tensor_scalar", out=e8, in0=e8, scalar1=rec,
             scalar2=None, op0=OP.mult)
        P.drain("vector")
        P.wait("vector", "vv", st["bias8_vv"])
        P.op("vector", "tensor_tensor", out=b8, in0=e8, in1=bias8, op=OP.add)
        P.drain("vector")
        P.op("vector", "max", out=m8, in_=b8)
        P.drain("vector")
        P.op("vector", "tensor_scalar", out=msk, in0=b8, scalar1=m8[:, 1:2],
             scalar2=None, op0=OP.is_ge)
        P.drain("vector")
        P.op("vector", "tensor_tensor", sem="vv", out=gtm, in0=e8, in1=msk,
             op=OP.mult)
        P.waitc("tensor", "vv")
        tb = P.op("tensor", "transpose", G_ps[0:8, sub * 128:(sub + 1) * 128],
                  gtm, g["identS"][:, :], sem="mm")
        P.wait("vector", "mm", tb)
        P.drain("vector")
        P.op("vector", "tensor_copy", sem="vv",
             out=g["gates_fm"][0:8, nt * TT + sub * 128: nt * TT + (sub + 1) * 128],
             in_=G_ps[0:8, sub * 128:(sub + 1) * 128])
    g["gates_vv"] = P.snap("vv")



def wsrc_ap(g, kind, e, idx):
    dr = g["dr"]
    if kind in ("w1", "w3"):
        if e < E:
            return bass.AP(dr[kind + "t"], e * D * D + idx * 512,
                           [[D, 128], [128 * D, DC], [1, 512]])
        return bass.AP(dr["s" + kind + "t"], (e - E) * 1024 + idx * 512,
                       [[ISH, 128], [128 * ISH, DC], [1, 512]])
    if e < E:
        return bass.AP(dr["w2t"], e * D * D + idx * 256,
                       [[D, 128], [128 * D, DC], [1, 256]])
    return bass.AP(dr["sw2t"], (e - E) * 8 * 128 * D + idx * 256,
                   [[D, 128], [128 * D, DC], [1, 256]])


def stash_ap(g, kind, e, idx):
    n = 4096 if kind in ("w1", "w3") else 2048
    return g[kind + "s"][e, :, idx * n:(idx + 1) * n]


def wslot(g, kind, slot):
    if kind == "w1":
        return av(g, AW1 + slot * 4096, 4096)
    if kind == "w3":
        return av(g, AW3 + slot * 4096, 4096)
    return av(g, AW2 + slot * 2048, 2048)


def emit_wload(P, g, app, kind, e, idx, slot, wait_mm=None, wait_vv=None):
    sv = wslot(g, kind, slot)
    gsem = {"w1": "gw1", "w3": "gw3", "w2": "gw2"}[kind]
    ssem = {"w1": "sw1", "w3": "sw3", "w2": "sw2"}[kind]
    tsem = {"w1": "tw1", "w3": "tw3", "w2": "tw2"}[kind]
    st = g["_st"]
    if app == 0:
        P.wait("gpsimd", "mm", wait_mm)
        P.wait("gpsimd", "vv", wait_vv)
        P.wait("gpsimd", tsem, st.get(("stw", kind, slot), 0))
        P.waitc("gpsimd", gsem)
        dsh = sv.rearrange("p (a b) -> p a b", a=DC)
        P.dma("gpsimd", dsh, wsrc_ap(g, kind, e, idx), gsem)
        v = P.snap(gsem)
        P.wait("sync", gsem, v)
        P.waitc("sync", tsem)
        P.dma("sync", stash_ap(g, kind, e, idx), sv, tsem)
        st[("stw", kind, slot)] = P.snap(tsem)
        return (gsem, v)
    P.wait("sync", "mm", wait_mm)
    P.wait("sync", "vv", wait_vv)
    P.waitc("sync", ssem)
    P.dma("sync", sv, stash_ap(g, kind, e, idx), ssem)
    return (ssem, P.snap(ssem))


def psum_fence(P):
    P.wait("tensor", "ss", P.snap("ss"))
    P.wait("tensor", "vv", P.snap("vv"))


def emit_ffn_sublayer(P, g, app, nex=NEX):
    emit_ln(P, g, with_gate=True)
    st = g["_st"]
    scr = g["scr"]
    hpairs = [(g["pbig0"][:, 0:512], g["pbig0"][:, 512:1024]),
              (g["pbig1"][:, 0:512], g["pbig1"][:, 512:1024]),
              (g["p4"][:, :], g["p5"][:, :])]
    Gb = g["p6"]
    yb = [g["p4"], g["p5"], g["p7"]]
    ones1_bf = g["ones_bfS"][0:1, :]
    cvring = []
    yring = []
    silu_ring = [scr[:, 0:256].bitcast(BF16), scr[:, 256:512].bitcast(BF16),
                 scr[:, 512:768].bitcast(BF16)]
    tmp_ring = [scr[:, 768:1024].bitcast(BF16), scr[:, 1024:1280].bitcast(BF16)]
    uc = [0]

    psum_fence(P)
    first_mm = P.snap("mm")
    first_vv = P.snap("vv")
    ld = {}
    if app == 1 and "preld" in g:
        ld.update(g.pop("preld"))
    else:
        for kind, idx, slot in [("w1", 0, 0), ("w3", 0, 0), ("w2", 0, 0),
                                ("w1", 1, 1), ("w3", 1, 1), ("w2", 1, 1)]:
            ld[(0, kind, idx)] = emit_wload(P, g, app, kind, 0, idx, slot,
                                            wait_mm=first_mm, wait_vv=first_vv)

    for ei in range(nex):
        is_sh = ei >= E
        psum_fence(P)
        for nt in range(NTT):
            if not is_sh:
                P.wait("tensor", "vv", g["gates_vv"])
                P.wait("tensor", "gg", st.get("gbc_used", 0))
                P.wait("tensor", "ss", st.get("gbc_evac", 0))
                gbv = P.op("tensor", "matmul", Gb[:, :],
                           g["sel8S"][:, ei * 128:(ei + 1) * 128],
                           g["gates_fm"][0:8, nt * TT:(nt + 1) * TT],
                           start=True, stop=True, sem="mm")
                st["gbc_mm"] = gbv
                P.wait("scalar", "mm", gbv)
                P.wait("scalar", "gg", st.get("gbc_used", 0))
                P.drain("scalar")
                st["gbc_evac"] = P.op("scalar", "copy", sem="ss",
                                      out=g["smf"][:, 0:512], in_=Gb[:, :])
            for ic in range(DC):
                u = uc[0]; uc[0] += 1
                hb, h3 = hpairs[u % 3]
                hf, icl = ic // 4, ic % 4
                w1v, w3v = wslot(g, "w1", hf), wslot(g, "w3", hf)
                s1, v1 = ld[(ei, "w1", hf)]
                s3, v3 = ld[(ei, "w3", hf)]
                P.wait("tensor", s1, v1)
                P.wait("tensor", s3, v3)
                if len(cvring) >= 3:
                    P.wait("tensor", "vv", cvring[-3][0])
                    P.wait("tensor", "ss", cvring[-3][2])
                for kc in range(DC):
                    P.op("tensor", "matmul", hb,
                         w1v[:, kc * 512 + icl * 128: kc * 512 + (icl + 1) * 128],
                         av(g, AXT + kc * NT + nt * TT, TT),
                         start=(kc == 0), stop=(kc == DC - 1))
                mv = None
                for kc in range(DC):
                    mv = P.op("tensor", "matmul", h3,
                              w3v[:, kc * 512 + icl * 128: kc * 512 + (icl + 1) * 128],
                              av(g, AXT + kc * NT + nt * TT, TT),
                              start=(kc == 0), stop=(kc == DC - 1),
                              sem="mm" if kc == DC - 1 else None)
                silu = silu_ring[u % 3]
                P.wait("scalar", "mm", mv)
                P.drain("scalar")
                sse = P.op("scalar", "activation", sem="ss", out=silu, in_=hb,
                           func=AF.Silu)
                hout = av(g, AH + ic * NT + nt * TT, TT)
                P.waitc("vector", "ss")
                if not is_sh:
                    last2 = (nt == NTT - 1 and ic >= DC - 2)
                    tmp = tmp_ring[u % 2]
                    if len(cvring) >= 2 and cvring[-2][1]:
                        P.wait("vector", "gg", cvring[-2][1])
                    P.drain("vector")
                    tv = P.op("vector", "tensor_tensor", sem="vv", out=tmp,
                              in0=h3, in1=silu, op=OP.mult)
                    if last2:
                        P.wait("vector", "ss", st["gbc_evac"])
                        if len(cvring) >= 2 and cvring[-2][1]:
                            P.wait("vector", "gg", cvring[-2][1])
                        P.drain("vector")
                        cv = P.op("vector", "tensor_tensor", sem="vv",
                                  out=hout, in0=tmp, in1=g["smf"][:, 0:512],
                                  op=OP.mult)
                        cvring.append((cv, None, sse))
                    else:
                        P.wait("gpsimd", "vv", tv)
                        P.wait("gpsimd", "ss", st["gbc_evac"])
                        if len(cvring) >= 2:
                            P.wait("gpsimd", "gg", cvring[-2][1])
                        cv = P.op("gpsimd", "tensor_tensor", sem="gg", out=hout,
                                  in0=tmp, in1=g["smf"][:, 0:512], op=OP.mult)
                        cvring.append((tv, cv, sse))
                else:
                    P.drain("vector")
                    cv = P.op("vector", "tensor_tensor", sem="vv", out=hout,
                              in0=h3, in1=silu, op=OP.mult)
                    cvring.append((cv, None, sse))
            st["gbc_used"] = P.snap("gg")
        a_done_mm = P.snap("mm")
        h_done_vv = P.snap("vv")
        h_done_gg = P.snap("gg")
        if ei + 1 < nex:
            for kind in ("w1", "w3"):
                for hf in range(2):
                    ld[(ei + 1, kind, hf)] = emit_wload(
                        P, g, app, kind, ei + 1, hf, hf, wait_mm=a_done_mm)
        psum_fence(P)
        P.wait("tensor", "gg", P.snap("gg"))
        for dcx in range(DC):
            qv = dcx // 2
            s2, v2 = ld[(ei, "w2", qv)]
            for nt in range(NTT):
                ybank = yb[(dcx * NTT + nt) % 3]
                P.wait("tensor", s2, v2)
                P.wait("tensor", "vv", h_done_vv)
                P.wait("tensor", "gg", h_done_gg)
                if len(yring) >= 3:
                    P.wait("tensor", "vv", yring[-3])
                mv = None
                for icx in range(DC):
                    mv = P.op("tensor", "matmul", ybank[:, :],
                              wslot(g, "w2", qv % 2)[:, icx * 256 + (dcx % 2) * 128:
                                                     icx * 256 + (dcx % 2) * 128 + 128],
                              av(g, AH + icx * NT + nt * TT, TT),
                              start=(icx == 0), stop=(icx == DC - 1),
                              sem="mm" if icx == DC - 1 else None)
                P.wait("vector", "mm", mv)
                P.drain("vector")
                yv = P.op("vector", "scalar_tensor_tensor", sem="vv",
                          out=xr(g, dcx, nt * TT, (nt + 1) * TT), in0=ybank[:, :],
                          scalar=0.5, in1=xr(g, dcx, nt * TT, (nt + 1) * TT),
                          op0=OP.mult, op1=OP.add)
                yring.append(yv)
            if dcx % 2 == 1 and dcx < 4:
                nq = dcx // 2 + 2
                ld[(ei, "w2", nq)] = emit_wload(P, g, app, "w2", ei, nq,
                                                nq % 2, wait_mm=P.snap("mm"))
            if dcx == DC - 1 and ei + 1 < nex:
                for q2 in range(2):
                    ld[(ei + 1, "w2", q2)] = emit_wload(
                        P, g, app, "w2", ei + 1, q2, q2, wait_mm=P.snap("mm"))
    g["xres_writer_vv"] = P.snap("vv")



def emit_attention(P, g):
    emit_ln(P, g, with_gate=False, bounce=g["xt_bounce"])
    st = g["_st"]
    scr, smf = g["scr"], g["smf"]

    P.waitc("gpsimd", "bnc")
    P.waitc("gpsimd", "scd")
    P.waitc("sync", "tw1")
    P.waitc("sync", "tw3")
    P.waitc("sync", "tw2")
    P.waitc("sync", "gw1")
    P.waitc("sync", "gw3")
    P.waitc("sync", "gw2")
    P.op("gpsimd", "collective_compute", "AllGather", OP.bypass,
         replica_groups=RG, ins=[g["xt_bounce"][:, :].opt()],
         outs=[g["xt_gath"][:, :].opt()], sem="cc")
    g["ag_cc"] = P.snap("cc")
    if not g["_st"].get("sel2_loaded"):
        g["_st"]["sel2_loaded"] = True
        P.dma("sync", avr(g, ASP + 1408, 256, 0, 33).bitcast(F32),
              g["dr"]["sel2"][:, :], "sl2")
        g["sel2_v"] = P.snap("sl2")

    P.wait("gpsimd", "mm", P.snap("mm"))
    P.waitc("gpsimd", "tw1")
    P.waitc("gpsimd", "tw3")
    P.waitc("gpsimd", "tw2")
    wk = av(g, AWS, 4096)
    src = bass.AP(g["dr"]["wdkv"], 0, [[LAT, 128], [128 * LAT, DC], [1, LAT]])
    P.dma("gpsimd", wk[:, 0:DC * LAT].rearrange("p (a b) -> p a b", a=DC),
          src, "gwa")
    wdkv_gld = P.snap("gwa")

    lat_ps = [g["p4"], g["p5"]]
    psum_fence(P)
    xt_used = [0, 0]
    W2 = 256
    for ntg in range(2 * NQG):
        half = ntg // 8
        xti = av(g, AXTI + (ntg % 2) * 2048, 2048)
        xsem = "xta" if ntg % 2 == 0 else "xtb"
        P.wait("sync", "cc", g["ag_cc"])
        P.wait("sync", "mm", xt_used[ntg % 2])
        src = bass.AP(g["xt_gath"], half * D * NT + (ntg % 8) * W2,
                      [[NT, 128], [128 * NT, DC], [1, W2]])
        P.dma("sync", xti.rearrange("p (c j) -> p c j", c=DC), src, xsem)
        xt_sld = P.snap(xsem)
        for lc in range(LC):
            P.wait("tensor", xsem, xt_sld)
            P.wait("tensor", "gwa", wdkv_gld)
            P.wait("tensor", "vv", st.get(("lat_evac", ntg - 1, lc), 0))
            for c in range(DC):
                P.op("tensor", "matmul", lat_ps[lc][:, 0:W2],
                     wk[:, c * LAT + lc * 128: c * LAT + (lc + 1) * 128],
                     xti[:, c * W2:(c + 1) * W2],
                     start=(c == 0), stop=(c == DC - 1),
                     sem="mm" if c == DC - 1 else None)
            st[("lat_mm", ntg, lc)] = P.snap("mm")
        xt_used[ntg % 2] = P.snap("mm")
        emit_lat_ln(P, g, ntg, lat_ps, W2)
    st[("xti_used", 0)] = xt_used[0]
    st[("xti_used", 1)] = xt_used[1]
    g["lat_vv"] = P.snap("vv")
    st["wsl_free_mm"] = P.snap("mm")

    for p in range(2):
        emit_attn_pass(P, g, p)

    if g["stage"] == "full":
        pmm, pvv = P.snap("mm"), P.snap("vv")
        preld = {}
        for kind, idx, slot in [("w1", 0, 0), ("w3", 0, 0), ("w2", 0, 0),
                                ("w1", 1, 1), ("w3", 1, 1), ("w2", 1, 1)]:
            preld[(0, kind, idx)] = emit_wload(P, g, 1, kind, 0, idx, slot,
                                               wait_mm=pmm, wait_vv=pvv)
        g["preld"] = preld

    P.waitc("gpsimd", "bna")
    P.waitc("gpsimd", "bnb")
    P.op("gpsimd", "collective_compute", "ReduceScatter", OP.add,
         replica_groups=RG, ins=[g["y_bounce"][:, :].opt()],
         outs=[g["y_red"][:, :].opt()], sem="cc")
    rs_cc = P.snap("cc")
    ytA = av(g, ALAT, 512)
    ytB = av(g, ALAT + 512, 512)
    for dcx in range(DC):
        for nt in range(NTT):
            par = (dcx * NTT + nt) % 2
            yt = ytA if par == 0 else ytB
            ysem = "yra" if par == 0 else "yrb"
            P.wait("sync", "cc", rs_cc)
            P.wait("sync", "vv", st.get(("yred_used", par), 0))
            src = bass.AP(g["y_red"], dcx * 128 * NT + nt * TT,
                          [[NT, 128], [1, TT]])
            P.dma("sync", yt, src, ysem)
            ysld = P.snap(ysem)
            P.wait("vector", ysem, ysld)
            P.drain("vector")
            yv = P.op("vector", "tensor_tensor", sem="vv",
                      out=xr(g, dcx, nt * TT, (nt + 1) * TT),
                      in0=xr(g, dcx, nt * TT, (nt + 1) * TT), in1=yt,
                      op=OP.add)
            st[("yred_used", (dcx * NTT + nt) % 2)] = yv
    g["xres_writer_vv"] = P.snap("vv")


def emit_lat_ln(P, g, ntg, lat_ps, W):
    st = g["_st"]
    scr = g["scr"]
    lraw = [av(g, ASP, W), av(g, ASP + W, W)]
    sq = av(g, ASP + 2 * W, W)
    S2, Q2 = g["p6"], g["p7"]
    for lc in range(LC):
        P.wait("vector", "mm", st[("lat_mm", ntg, lc)])
        P.drain("vector")
        P.op("vector", "tensor_copy", sem="vv", out=lraw[lc],
             in_=lat_ps[lc][:, 0:W])
    lraw_vv = P.snap("vv")
    sqv = {}
    for lc in range(LC):
        P.wait("scalar", "vv", lraw_vv)
        if lc == 1:
            P.wait("scalar", "mm", sqv[0])
        P.drain("scalar")
        P.op("scalar", "activation", sem="ss", out=sq, in_=lraw[lc],
             func=AF.Square)
        sq_ss = P.snap("ss")
        P.wait("tensor", "vv", lraw_vv)
        P.wait("tensor", "ss", sq_ss)
        if lc == 0:
            P.wait("tensor", "vv", st.get("latln_prev_vv", 0))
        P.op("tensor", "matmul", S2[:, 0:W], g["ones_bfS"][:, :], lraw[lc],
             start=(lc == 0), stop=(lc == LC - 1))
        sqv[lc] = P.op("tensor", "matmul", Q2[:, 0:W], g["ones_bfS"][:, :], sq,
                       start=(lc == 0), stop=(lc == LC - 1), sem="mm")
    smm = P.snap("mm")
    mu2, var, xm, rstd = (scr[:, 0:W], scr[:, 512:512 + W],
                          scr[:, 1024:1024 + W], scr[:, 1536:1536 + W])
    P.wait("scalar", "mm", smm)
    P.drain("scalar")
    P.op("scalar", "activation", sem="ss", out=mu2, in_=S2[:, 0:W],
         func=AF.Square, scale=1.0 / LAT)
    P.waitc("vector", "ss")
    P.wait("vector", "mm", smm)
    P.drain("vector")
    P.op("vector", "scalar_tensor_tensor", sem="vv", out=var, in0=Q2[:, 0:W],
         scalar=1.0 / LAT, in1=mu2, op0=OP.mult, op1=OP.subtract)
    P.waitc("scalar", "vv")
    P.drain("scalar")
    P.op("scalar", "activation", sem="ss", out=rstd, in_=var, func=AF.Ln,
         bias=g["ccolS"][:, 0:1], scale=1.0)
    P.drain("scalar")
    P.op("scalar", "activation", sem="ss", out=rstd, in_=rstd, func=AF.Exp,
         scale=-0.5)
    P.waitc("vector", "ss")
    for lc in range(LC):
        P.drain("vector")
        P.op("vector", "scalar_tensor_tensor", out=xm, in0=S2[:, 0:W],
             scalar=-1.0 / LAT, in1=lraw[lc], op0=OP.mult, op1=OP.add)
        P.drain("vector")
        P.op("vector", "tensor_tensor", out=xm, in0=xm, in1=rstd, op=OP.mult)
        P.drain("vector")
        ev = P.op("vector", "tensor_scalar", sem="vv",
                  out=av(g, ALAT + lc * T + ntg * W, W), in0=xm,
                  scalar1=g["lng_sb"][:, lc:lc + 1],
                  scalar2=g["lnb_sb"][:, lc:lc + 1], op0=OP.mult, op1=OP.add)
        st[("lat_evac", ntg, lc)] = ev
    st["latln_prev_vv"] = P.snap("vv")


def emit_rope(P, g, src_ps, mv, dst_bf, tcol0, W):
    scr = g["scr"]
    qsw = scr[:, 0:W]
    t1 = scr[:, 512:512 + W]
    mask = [(i ^ 1) for i in range(32)]
    P.wait("vector", "mm", mv)
    P.drain("vector")
    P.op("vector", "stream_shuffle", out=qsw, in_=src_ps[:, 0:W], mask=mask)
    P.op("vector", "tensor_tensor", out=t1, in0=src_ps[:, 0:W],
         in1=g["cos2S"][:, tcol0:tcol0 + W], op=OP.mult)
    P.drain("vector")
    P.op("vector", "tensor_tensor", out=qsw, in0=qsw,
         in1=g["sin2S"][:, tcol0:tcol0 + W], op=OP.mult)
    P.drain("vector")
    return P.op("vector", "tensor_tensor", sem="vv", out=dst_bf, in0=t1,
                in1=qsw, op=OP.add)


def emit_norm_acc(P, g, which, ntg, acc, abase, col0, W):
    st = g["_st"]
    smf = g["smf"]
    sqs = av(g, ASP, W)
    N_ps = g["p6"]
    mv = None
    for qc in range(2):
        P.wait("scalar", "vv", st[(which + "_done", ntg, qc)])
        P.wait("scalar", "mm", st.get(("nq_mm", which), 0))
        P.drain("scalar")
        P.op("scalar", "activation", sem="ss", out=sqs,
             in_=av(g, abase + qc * T + col0, W), func=AF.Square)
        sq_ss = P.snap("ss")
        P.wait("tensor", "ss", sq_ss)
        if qc == 0:
            P.wait("tensor", "vv", st.get(("nred", which, ntg - 1), 0))
        mv = P.op("tensor", "matmul", N_ps[0:4, 0:W],
                  g["headonesS"][:, qc * 4:(qc + 1) * 4], sqs,
                  start=(qc == 0), stop=(qc == 1), sem="mm")
        st[("nq_mm", which)] = mv
    ntmax = smf[0:4, 20:21]
    P.wait("vector", "mm", mv)
    P.drain("vector")
    P.op("vector", "reduce_max", out=ntmax, in_=N_ps[0:4, 0:W], axis=AXX)
    P.drain("vector")
    if ntg == 0:
        rv = P.op("vector", "tensor_copy", sem="vv", out=acc, in_=ntmax)
    else:
        rv = P.op("vector", "tensor_tensor", sem="vv", out=acc, in0=acc,
                  in1=ntmax, op=OP.max)
    st[("nred", which, ntg)] = rv


def emit_bias(P, g, qmax, kmax):
    st = g["_st"]
    smf = g["smf"]
    t = smf[0:4, 21:22]
    P.drain("vector")
    tv = P.op("vector", "tensor_tensor", sem="vv", out=t, in0=qmax, in1=kmax,
              op=OP.mult)
    P.wait("scalar", "vv", tv)
    P.drain("scalar")
    sv = P.op("scalar", "activation", sem="ss", out=t, in_=t, func=AF.Sqrt)
    P.wait("vector", "ss", sv)
    P.drain("vector")
    bv = P.op("vector", "tensor_scalar", sem="vv", out=t, in0=t,
              scalar1=-0.125, scalar2=0.25, op0=OP.mult, op1=OP.subtract)
    P.wait("tensor", "vv", bv)
    P.wait("tensor", "vv", st["v_done"])
    tr = P.op("tensor", "transpose", g["p7"][0:1, 0:4], t,
              g["identS"][0:4, 0:4], sem="mm")
    P.wait("vector", "mm", tr)
    P.drain("vector")
    rv = P.op("vector", "tensor_copy", sem="vv", out=smf[0:1, 24:28],
              in_=g["p7"][0:1, 0:4])
    P.wait("tensor", "vv", rv)
    bb = P.op("tensor", "matmul", g["p7"][:, 8:12], g["onesrow_fS"][:, :],
              smf[0:1, 24:28], start=True, stop=True, sem="mm")
    P.wait("vector", "mm", bb)
    P.drain("vector")
    P.op("vector", "tensor_copy", sem="vv", out=smf[:, 16:20],
         in_=g["p7"][:, 8:12])
    st["bias_vv"] = P.snap("vv")
    P.wait("scalar", "vv", st["bias_vv"])


def emit_attn_pass(P, g, p):
    st = g["_st"]
    scr, smf = g["scr"], g["smf"]
    wsl = av(g, AWS, 4096)
    qmax = smf[0:4, 8:9]
    kmax = smf[0:4, 9:10]

    P.wait("gpsimd", "mm", st["wsl_free_mm"])
    src = bass.AP(g["dr"]["wq_sh"], 0,
                  [[HL * DH, 128], [128 * HL * DH, DC], [1, HL * DH]])
    P.dma("gpsimd", wsl.rearrange("p (a b) -> p a b", a=DC), src, "gwa")
    wq_gld = P.snap("gwa")

    q_ps = [g["p4"], g["p5"]]
    psum_fence(P)
    W2 = 256
    for ntg in range(2 * NQG):
        half = ntg // 8
        xti = av(g, AXTI + (ntg % 2) * 2048, 2048)
        xsem = "xta" if ntg % 2 == 0 else "xtb"
        P.wait("sync", "cc", g["ag_cc"])
        P.wait("sync", "mm", st.get(("xti_used", ntg % 2), 0))
        src = bass.AP(g["xt_gath"], half * D * NT + (ntg % 8) * W2,
                      [[NT, 128], [128 * NT, DC], [1, W2]])
        P.dma("sync", xti.rearrange("p (c j) -> p c j", c=DC), src, xsem)
        xt_sld = P.snap(xsem)
        for qc in range(2):
            P.wait("tensor", xsem, xt_sld)
            P.wait("tensor", "gwa", wq_gld)
            P.wait("tensor", "vv", st.get(("q_done", ntg - 1, qc), 0))
            mv = None
            for c in range(DC):
                mv = P.op("tensor", "matmul", q_ps[qc][:, 0:W2],
                          wsl[:, c * 512 + (p * 2 + qc) * 128:
                              c * 512 + (p * 2 + qc) * 128 + 128],
                          xti[:, c * W2:(c + 1) * W2],
                          start=(c == 0), stop=(c == DC - 1),
                          sem="mm" if c == DC - 1 else None)
            st[("q_done", ntg, qc)] = emit_rope(
                P, g, q_ps[qc], mv, av(g, AQ + qc * T + ntg * W2, W2),
                ntg * W2, W2)
        st[("xti_used", ntg % 2)] = P.snap("mm")
        emit_norm_acc(P, g, "q", ntg, qmax, AQ, ntg * W2, W2)
    q_all_mm = P.snap("mm")

    P.wait("gpsimd", "mm", q_all_mm)
    src = bass.AP(g["dr"]["wuk_sh"], 0,
                  [[HL * DH, 128], [128 * HL * DH, LC], [1, HL * DH]])
    P.dma("gpsimd", wsl[:, 0:LC * HL * DH].rearrange("p (a b) -> p a b", a=LC),
          src, "gwa")
    wuk_gld = P.snap("gwa")
    for ntg in range(NQG):
        for kc in range(2):
            P.wait("tensor", "gwa", wuk_gld)
            P.wait("tensor", "vv", g["lat_vv"])
            if ntg == 0:
                P.wait("tensor", "vv", st[("q_done", NQG - 1, kc)])
            P.wait("tensor", "vv", st.get(("k_done", ntg - 1, kc), 0))
            mv = None
            for lc in range(LC):
                mv = P.op("tensor", "matmul", q_ps[kc][:, :],
                          wsl[:, lc * 512 + (p * 2 + kc) * 128:
                              lc * 512 + (p * 2 + kc) * 128 + 128],
                          av(g, ALAT + lc * T + ntg * TT, TT),
                          start=(lc == 0), stop=(lc == LC - 1),
                          sem="mm" if lc == LC - 1 else None)
            st[("k_done", ntg, kc)] = emit_rope(
                P, g, q_ps[kc], mv, av(g, AK + kc * T + ntg * TT, TT),
                ntg * TT, TT)
        emit_norm_acc(P, g, "k", ntg, kmax, AK, ntg * TT, TT)
    k_all_mm = P.snap("mm")

    P.wait("gpsimd", "mm", k_all_mm)
    src = bass.AP(g["dr"]["wuv_sh"], 0,
                  [[HL * DH, 128], [128 * HL * DH, LC], [1, HL * DH]])
    P.dma("gpsimd", wsl[:, 0:LC * HL * DH].rearrange("p (a b) -> p a b", a=LC),
          src, "gwa")
    wuv_gld = P.snap("gwa")
    P.wait("vector", "vv", st.get("v_used", 0))
    P.drain("vector")
    ones_vv = P.op("vector", "memset",
                   av(g, AV, 32 * 260).rearrange("p (kt h j) -> p kt h j",
                                                 kt=32, h=4)[:, :, :, 64:65],
                   1.0, sem="vv")
    vb2 = [g["p7"], g["p6"]]
    psum_fence(P)
    for kt in range(NKT):
        vb = vb2[kt % 2]
        P.wait("tensor", "gwa", wuv_gld)
        P.wait("tensor", "vv", g["lat_vv"])
        if kt < 2:
            P.wait("tensor", "vv", st.get(("nred", "k", NQG - 1), 0))
        P.wait("tensor", "vv", st.get(("v_evac", kt - 2), 0))
        mv = None
        for lc in range(LC):
            mv = P.op("tensor", "matmul", vb[:, 0:256],
                      av(g, ALAT + lc * T + kt * 128, 128),
                      wsl[:, lc * 512 + p * 256: lc * 512 + (p + 1) * 256],
                      start=(lc == 0), stop=(lc == LC - 1),
                      sem="mm" if lc == LC - 1 else None)
        P.wait("vector", "mm", mv)
        P.wait("vector", "vv", ones_vv)
        P.drain("vector")
        st[("v_evac", kt)] = P.op(
            "vector", "tensor_copy", sem="vv",
            out=av(g, AV + kt * 260, 260).rearrange("p (h j) -> p h j",
                                                    h=4)[:, :, 0:64],
            in_=vb[:, 0:256].rearrange("p (h j) -> p h j", h=4))
    st["v_done"] = P.snap("vv")

    emit_bias(P, g, qmax, kmax)

    P.waitc("scalar", "bna")
    P.waitc("scalar", "bnb")
    P.waitc("scalar", "ypa")
    P.waitc("scalar", "ypb")
    sgrp = [g["pbig0"], g["pbig1"], g["pbig2"]]
    obA, obB = g["p6"], g["p7"]
    psum_fence(P)
    if not st.get("lrinit"):
        st["lrinit"] = True
        P.drain("vector")
        P.op("vector", "memset", smf[0:33, 512:1024], 0.0)
    for pr in range(2):
        P.drain("vector")
        P.op("vector", "tensor_tensor", sem="vv", out=smf[:, 22 + pr:23 + pr],
             in0=smf[:, 16 + 2 * pr:17 + 2 * pr],
             in1=smf[:, 17 + 2 * pr:18 + 2 * pr], op=OP.min)
    pb_vv = P.snap("vv")
    P.wait("scalar", "vv", pb_vv)
    def norm_part2(P, g, pend):
        prq, qgq, lrss = pend
        P.wait("tensor", "ss", lrss)
        P.wait("tensor", "sl2", g["sel2_v"])
        P.wait("tensor", "vv", st.get("lbev_vv", 0))
        lb = P.op("tensor", "matmul", g["pbig2"][:, 0:512],
                  avr(g, ASP + 1408, 256, 0, 33).bitcast(F32),
                  smf[0:33, 512:1024], start=True, stop=True, sem="mm")
        lbev = scr[:, 1024:1536]
        P.wait("vector", "mm", lb)
        P.drain("vector")
        P.op("vector", "tensor_copy", sem="vv", out=lbev,
             in_=g["pbig2"][:, 0:512])
        P.drain("vector")
        P.op("vector", "tensor_tensor", sem="vv",
             out=avr(g, AO + prq * T + qgq * TT, TT, 0, 64),
             in0=scr[0:64, 0:512], in1=lbev[0:64, :], op=OP.mult)
        P.op("vector", "tensor_tensor", sem="vv",
             out=avr(g, AO + prq * T + qgq * TT, TT, 64, 128),
             in0=scr[64:128, 512:1024], in1=lbev[64:128, :], op=OP.mult)
        st["lbev_vv"] = P.snap("vv")

    for pr in range(2):
        hA, hB = 2 * pr, 2 * pr + 1
        for qg in range(NQG):
            nkt = 4 * (qg + 1)
            for step in range(nkt + 2):
                if step >= 2:
                    kt = step - 2
                    m0 = max(0, kt - 4 * qg)
                    ptv = av(g, APT + (kt % 2) * 1024, 1024)
                    rsem, rval = st[("pt_ready", kt)]
                    P.wait("tensor", rsem, rval)
                    if kt == 0:
                        P.wait("tensor", "vv", st.get("o_free", 0))
                        P.wait("tensor", "vv", st.get("lbev_vv", 0))
                    P.op("tensor", "matmul", obA[0:65, m0 * 128:512],
                         av(g, AV + kt * 260 + hA * 65, 65),
                         ptv[:, m0 * 128:512],
                         start=(kt == 0), stop=(kt == nkt - 1))
                    avmm = P.op("tensor", "matmul", obB[0:65, m0 * 128:512],
                                av(g, AV + kt * 260 + hB * 65, 65),
                                ptv[:, 512 + m0 * 128:1024],
                                start=(kt == 0), stop=(kt == nkt - 1), sem="mm")
                    st[("pt_av", kt % 2)] = avmm
                if step == 2 and st.get("pending_norm"):
                    norm_part2(P, g, st.pop("pending_norm"))
                if step < nkt:
                    kt = step
                    gi = kt % 3
                    grp = sgrp[gi]
                    m0 = max(0, kt - 4 * qg)
                    P.wait("tensor", "ss", st.get(("exp_ss", gi), 0))
                    if kt == 0:
                        P.wait("tensor", "vv", st["v_done"])
                    if gi == 2:
                        P.wait("tensor", "vv", st.get("lbev_vv", 0))
                    P.op("tensor", "matmul", grp[:, 0:512],
                         avr(g, AK + pr * T + kt * 128, 128, 0, 64),
                         avr(g, AQ + pr * T + qg * TT, TT, 0, 64),
                         start=True, stop=True, tile_position=(0, 0))
                    smm = P.op("tensor", "matmul", grp[:, 512:1024],
                               avr(g, AK + pr * T + kt * 128, 128, 64, 128),
                               avr(g, AQ + pr * T + qg * TT, TT, 64, 128),
                               start=True, stop=True, sem="mm",
                               tile_position=(64, 0))
                    ptv = av(g, APT + (kt % 2) * 1024, 1024)
                    P.wait("scalar", "mm", smm)
                    P.wait("scalar", "mm", st.get(("pt_av", kt % 2), 0))
                    P.wait("scalar", "vv", st.get(("pt_msk", kt % 2), 0))
                    P.drain("scalar")
                    gv = grp[:, :].rearrange("p (h q) -> p h q", h=2)
                    pv = ptv.rearrange("p (h q) -> p h q", h=2)
                    es_ = P.op("scalar", "activation", sem="ss",
                               out=pv[:, :, m0 * 128:512],
                               in_=gv[:, :, m0 * 128:512], func=AF.Exp,
                               bias=smf[:, 22 + pr:23 + pr], scale=0.125)
                    st[("exp_ss", gi)] = es_
                    if kt >= 4 * qg:
                        P.wait("vector", "ss", es_)
                        P.drain("vector")
                        P.op("vector", "tensor_tensor",
                             out=ptv[:, m0 * 128:512],
                             in0=ptv[:, m0 * 128:512],
                             in1=g["masksS"][:, m0 * 512 + m0 * 128:
                                             (m0 + 1) * 512],
                             op=OP.mult)
                        mk = P.op("vector", "tensor_tensor", sem="vv",
                                  out=ptv[:, 512 + m0 * 128:1024],
                                  in0=ptv[:, 512 + m0 * 128:1024],
                                  in1=g["masksS"][:, m0 * 512 + m0 * 128:
                                                  (m0 + 1) * 512],
                                  op=OP.mult)
                        st[("pt_msk", kt % 2)] = mk
                        st[("pt_ready", kt)] = ("vv", mk)
                    else:
                        st[("pt_ready", kt)] = ("ss", es_)
            av_mm = P.snap("mm")
            P.wait("vector", "mm", av_mm)
            P.wait("vector", "ss", st.get("lr_ss", 0))
            P.drain("vector")
            c1 = P.op("vector", "tensor_copy", sem="vv", out=scr[0:65, 0:512],
                      in_=obA[0:65, :])
            P.drain("vector")
            c2 = P.op("vector", "tensor_copy", sem="vv",
                      out=scr[32:33, 1536:2048], in_=obB[64:65, :])
            P.drain("vector")
            ofree = P.op("vector", "tensor_copy", sem="vv",
                         out=scr[64:128, 512:1024], in_=obB[0:64, :])
            st["o_free"] = ofree
            P.wait("scalar", "vv", c1)
            P.drain("scalar")
            P.op("scalar", "activation", out=smf[0:1, 512:1024],
                 in_=scr[64:65, 0:512], func=AF.Ln)
            P.wait("scalar", "vv", c2)
            P.op("scalar", "activation", out=smf[32:33, 512:1024],
                 in_=scr[32:33, 1536:2048], func=AF.Ln)
            P.drain("scalar")
            P.op("scalar", "activation", out=smf[0:1, 512:1024],
                 in_=smf[0:1, 512:1024], func=AF.Exp, scale=-1.0)
            lr_ss = P.op("scalar", "activation", sem="ss",
                         out=smf[32:33, 512:1024], in_=smf[32:33, 512:1024],
                         func=AF.Exp, scale=-1.0)
            st["lr_ss"] = lr_ss
            st["pending_norm"] = (pr, qg, lr_ss)
    if st.get("pending_norm"):
        norm_part2(P, g, st.pop("pending_norm"))
    o_done = P.snap("vv")
    st["v_used"] = o_done

    P.wait("gpsimd", "vv", o_done)
    wo = av(g, AV, 2048)
    src = bass.AP(g["dr"]["wo_sh"], p * 2 * 128 * D,
                  [[D, 128], [128 * D, 2], [1, D]])
    P.dma("gpsimd", wo.rearrange("p (a b) -> p a b", a=2), src, "gwa")
    wo_gld = P.snap("gwa")
    ytileA = av(g, APT, 512)
    ytileB = av(g, APT + 512, 512)
    yprev = av(g, APT + 1024, 512)
    yprev2 = av(g, APT + 1536, 512)
    P.wait("sync", "mm", P.snap("mm"))
    P.wait("sync", "vv", P.snap("vv"))
    psum_fence(P)
    yb3 = [g["p4"], g["p5"]]
    for ntg in range(NQG):
        for dcx in range(DC):
            i2 = (ntg * DC + dcx) % 2
            yb2 = yb3[i2]
            ytile = ytileA if i2 == 0 else ytileB
            P.wait("tensor", "gwa", wo_gld)
            P.wait("tensor", "vv", o_done)
            P.wait("tensor", "vv", st.get(("yt_used", i2), 0))
            mv = None
            for oc in range(2):
                mv = P.op("tensor", "matmul", yb2[:, :],
                          wo[:, oc * 1024 + dcx * 128: oc * 1024 + (dcx + 1) * 128],
                          av(g, AO + oc * T + ntg * TT, TT),
                          start=(oc == 0), stop=(oc == 1),
                          sem="mm" if oc == 1 else None)
            bsem = "bna" if i2 == 0 else "bnb"
            if p == 0:
                P.wait("vector", "mm", mv)
                P.wait("vector", bsem, st.get(("yt_sst", i2), 0))
                P.drain("vector")
                ytv = P.op("vector", "tensor_copy", sem="vv", out=ytile,
                           in_=yb2[:, :])
                st[("yt_used", i2)] = ytv
                P.wait("sync", "vv", ytv)
                dst = bass.AP(g["yacc"], dcx * 128 * T + ntg * TT,
                              [[T, 128], [1, TT]])
                P.dma("sync", dst, ytile, bsem)
                st[("yt_sst", i2)] = P.snap(bsem)
            else:
                ypsem = "ypa" if i2 == 0 else "ypb"
                P.wait("sync", "vv", st.get(("yp_used", i2), 0))
                src2 = bass.AP(g["yacc"], dcx * 128 * T + ntg * TT,
                               [[T, 128], [1, TT]])
                P.dma("sync", yprev if i2 == 0 else yprev2, src2, ypsem)
                ypl = P.snap(ypsem)
                P.wait("vector", "mm", mv)
                P.wait("vector", ypsem, ypl)
                P.wait("vector", bsem, st.get(("yt_sst", i2), 0))
                P.drain("vector")
                ytv = P.op("vector", "tensor_tensor", sem="vv", out=ytile,
                           in0=yb2[:, :], in1=yprev if i2 == 0 else yprev2,
                           op=OP.add)
                st[("yt_used", i2)] = ytv
                st[("yp_used", i2)] = ytv
                P.wait("sync", "vv", ytv)
                half = ntg // 4
                dst = bass.AP(g["y_bounce"],
                              (half * D + dcx * 128) * NT + (ntg % 4) * TT,
                              [[NT, 128], [1, TT]])
                P.dma("sync", dst, ytile, bsem)
                st[("yt_sst", i2)] = P.snap(bsem)
    st["wsl_free_mm"] = P.snap("mm")


def emit_output(P, g):
    if "dbg_gates" in g and g["stage"] == "ffn1":
        P.waitc("sync", "vv")
        P.dma("sync", g["dbg_gates"][:, :], g["gates_fm"][0:8, :], "outw")
        P.dma("sync", g["dbg_xt"][:, :], av(g, AXT, 2048), "outw")
        P.dma("sync", g["dbg_h"][:, :], av(g, AH, 2048), "outw")
    P.waitc("sync", "vv")
    for c in range(DC):
        dst = bass.AP(g["out"], c * 128 * NT, [[NT, 128], [1, NT]])
        P.dma("sync", dst, g["x_res"][:, c * NT:(c + 1) * NT], "outw")
    P.waitc("sync", "outw")



def rope_perm():
    perm = np.zeros(D, dtype=np.int64)
    for h in range(H):
        for j in range(DH // 2):
            perm[h * DH + 2 * j] = h * DH + j
            perm[h * DH + 2 * j + 1] = h * DH + DH // 2 + j
    return perm


def host_constants():
    c = {}
    c["ident"] = np.eye(128, dtype=np.float32)
    c["ones_bf"] = np.ones((128, 128), dtype=BF)
    c["onesrow_f"] = np.ones((1, 128), dtype=np.float32)
    ho = np.zeros((128, 8), dtype=BF)
    for qc in range(2):
        for hh in range(2):
            ho[hh * 64:(hh + 1) * 64, qc * 4 + 2 * qc + hh] = 1.0
    ho = np.zeros((128, 8), dtype=BF)
    for qc in range(2):
        for hh in range(2):
            m = 2 * qc + hh
            ho[hh * 64:(hh + 1) * 64, qc * 4 + m] = 1.0
    c["headones"] = ho
    half = DH // 2
    pos = np.arange(T, dtype=np.float64)
    inv = 1.0 / (10000.0 ** (np.arange(half, dtype=np.float64) / half))
    ang = pos[None, :] * inv[:, None]
    cosr = np.cos(ang)
    sinr = np.sin(ang)
    cos2 = np.zeros((128, T), dtype=np.float32)
    sin2 = np.zeros((128, T), dtype=np.float32)
    for r in range(128):
        j = (r % 64) // 2
        cos2[r] = cosr[j]
        sin2[r] = sinr[j] if (r % 2 == 1) else -sinr[j]
    c["cos2"] = cos2.astype(BF)
    c["sin2"] = sin2.astype(BF)
    masks = np.zeros((128, 4 * TT), dtype=BF)
    for m in range(4):
        kpos = m * 128 + np.arange(128)[:, None]
        qpos = np.arange(TT)[None, :]
        masks[:, m * TT:(m + 1) * TT] = (kpos <= qpos).astype(BF)
    c["masks"] = masks
    cc = np.zeros((128, 4), dtype=np.float32)
    cc[:, 0] = EPS
    cc[:, 1] = 1.0
    c["ccol"] = cc
    sel = np.zeros((8, E * 128), dtype=BF)
    for e in range(E):
        sel[e, e * 128:(e + 1) * 128] = 1.0
    c["sel8"] = sel
    s2 = np.zeros((33, 128), dtype=np.float32)
    s2[0, 0:64] = 1.0
    s2[32, 64:128] = 1.0
    c["sel2"] = s2
    return c


_NC_CACHE = {}


def kernel(L0_attn_x, L0_attn_W_q, L0_attn_W_dkv, L0_attn_W_uk, L0_attn_W_uv,
           L0_attn_W_o, L0_attn_ln_gamma, L0_attn_ln_beta, L0_ffn_gate_w,
           L0_ffn_gate_bias, L0_ffn_w1, L0_ffn_w3, L0_ffn_w2, L0_ffn_sw1,
           L0_ffn_sw3, L0_ffn_sw2, stage="full"):
    perm = rope_perm()
    wq_p = np.ascontiguousarray(np.asarray(L0_attn_W_q)[:, perm])
    wuk_p = np.ascontiguousarray(np.asarray(L0_attn_W_uk)[:, perm])
    con = host_constants()

    base = dict(con)
    base["w1t"] = np.ascontiguousarray(np.asarray(L0_ffn_w1).transpose(0, 2, 1))
    base["w3t"] = np.ascontiguousarray(np.asarray(L0_ffn_w3).transpose(0, 2, 1))
    base["w2t"] = np.ascontiguousarray(np.asarray(L0_ffn_w2).transpose(0, 2, 1))
    base["sw1t"] = np.ascontiguousarray(np.asarray(L0_ffn_sw1).T)
    base["sw3t"] = np.ascontiguousarray(np.asarray(L0_ffn_sw3).T)
    base["sw2t"] = np.ascontiguousarray(np.asarray(L0_ffn_sw2).T)
    base["gwt"] = np.ascontiguousarray(np.asarray(L0_ffn_gate_w).T)
    base["gbias"] = np.asarray(L0_ffn_gate_bias).reshape(1, E).astype(np.float32)
    base["wdkv"] = np.ascontiguousarray(np.asarray(L0_attn_W_dkv))
    lg = np.asarray(L0_attn_ln_gamma).reshape(LC, 128).T
    lb = np.asarray(L0_attn_ln_beta).reshape(LC, 128).T
    base["ln_g"] = np.ascontiguousarray(lg.astype(np.float32))
    base["ln_b"] = np.ascontiguousarray(lb.astype(np.float32))

    x = np.asarray(L0_attn_x)
    x_fm = x.transpose(0, 2, 1)

    in_maps = []
    for core in range(8):
        b, hh = core // 2, core % 2
        m = dict(base)
        m["x_sh"] = np.ascontiguousarray(x_fm[b, :, hh * NT:(hh + 1) * NT])
        hc = slice(hh * HL * DH, (hh + 1) * HL * DH)
        m["wq_sh"] = np.ascontiguousarray(wq_p[:, hc])
        m["wuk_sh"] = np.ascontiguousarray(wuk_p[:, hc])
        m["wuv_sh"] = np.ascontiguousarray(np.asarray(L0_attn_W_uv)[:, hc])
        m["wo_sh"] = np.ascontiguousarray(np.asarray(L0_attn_W_o)[hc, :])
        in_maps.append(m)

    key = stage
    if key not in _NC_CACHE:
        _NC_CACHE[key] = build_program(stage)
    nc = _NC_CACHE[key]
    res = run_bass_kernel_spmd(nc, in_maps, core_ids=list(range(8)))
    outs = []
    for b in range(BATCH):
        lo = res.results[2 * b]["out"]
        hi = res.results[2 * b + 1]["out"]
        outs.append(np.concatenate([lo, hi], axis=1).T)
    return np.stack(outs).astype(np.float32)


# version 15
# speedup vs baseline: 1.0588x; 1.0588x over previous
import numpy as np
import ml_dtypes
from contextlib import ExitStack

import concourse.bass as bass
import concourse.mybir as mybir
from concourse.alu_op_type import AluOpType as OP
from concourse.bass_utils import run_bass_kernel_spmd

DEBUG_RACES = False

F32 = mybir.dt.float32
BF16 = mybir.dt.bfloat16
AF = mybir.ActivationFunctionType
AXX = mybir.AxisListType.X
BF = ml_dtypes.bfloat16

D = 1024
T = 4096
BATCH = 4
H = 16
DH = 64
LAT = 256
E = 8
ISH = 2048
NT = 2048
TT = 512
NTT = NT // TT
DC = D // 128
LC = LAT // 128
HL = H // 2
EPS = 1e-5
NQG = T // TT
NKT = T // 128
NEX = E + 2

RG = [[0, 1], [2, 3], [4, 5], [6, 7]]

AXT = 0
AH = 16384
ALN = 16384
AW1 = 32768
AW3 = 40960
AW2 = 49152
ARENA = 53248
ALAT = 0
AQ = 8192
AK = 16384
AV = 24576
APT = 32896
AO = 34944
AWS = 43136
AXTI = 47232
ASP = 51328


class Pass:

    def __init__(self, which, eng, semh):
        self.which = which
        self.eng = eng
        self.semh = semh
        self.c = {k: 0 for k in semh}

    def op(self, engname, fname, *args, sem=None, inc=1, **kw):
        if sem is not None:
            self.c[sem] += inc
        if self.which == engname:
            inst = getattr(self.eng, fname)(*args, **kw)
            if sem is not None:
                inst.then_inc(self.semh[sem], inc)
        return self.c[sem] if sem is not None else None

    def dma(self, engname, out, in_, sem):
        return self.op(engname, "dma_start", sem=sem, inc=16, out=out, in_=in_)

    def wait(self, engname, sem, val):
        if val is None or val <= 0:
            return
        if self.which == engname:
            self.eng.wait_ge(self.semh[sem], val)

    def waitc(self, engname, sem):
        self.wait(engname, sem, self.c[sem])

    def drain(self, engname):
        if self.which == engname:
            self.eng.drain()

    def snap(self, sem):
        return self.c[sem]


def build_program(stage="full"):
    nc = bass.Bass("TRN2", target_bir_lowering=False,
                   detect_race_conditions=DEBUG_RACES)
    dr = {}

    def din(name, shape, dt=F32):
        dr[name] = nc.dram_tensor(name, list(shape), dt, kind="ExternalInput")

    din("x_sh", [D, NT])
    din("w1t", [E, D, D])
    din("w3t", [E, D, D])
    din("w2t", [E, D, D])
    din("sw1t", [D, ISH])
    din("sw3t", [D, ISH])
    din("sw2t", [ISH, D])
    din("gwt", [D, E])
    din("gbias", [1, E])
    din("wq_sh", [D, HL * DH])
    din("wdkv", [D, LAT])
    din("wuk_sh", [LAT, HL * DH])
    din("wuv_sh", [LAT, HL * DH])
    din("wo_sh", [HL * DH, D])
    din("ln_g", [128, LC])
    din("ln_b", [128, LC])
    din("ident", [128, 128])
    din("ones_bf", [128, 128], BF16)
    din("onesrow_f", [1, 128])
    din("headones", [128, 8], BF16)
    din("cos2", [128, T], BF16)
    din("sin2", [128, T], BF16)
    din("masks", [128, 4 * TT], BF16)
    din("ccol", [128, 4])
    din("sel8", [8, E * 128], BF16)
    din("sel2", [33, 128])

    out_t = nc.dram_tensor("out", [D, NT], F32, kind="ExternalOutput")
    if stage in ("ffn1",):
        g_dbg = nc.dram_tensor("dbg_gates", [8, NT], BF16, kind="ExternalOutput")
        x_dbg = nc.dram_tensor("dbg_xt", [128, 2048], BF16, kind="ExternalOutput")
        h_dbg = nc.dram_tensor("dbg_h", [128, 2048], BF16, kind="ExternalOutput")

    g0 = dict(dr=dr, out=out_t, stage=stage)
    if stage in ("ffn1",):
        g0["dbg_gates"] = g_dbg
        g0["dbg_xt"] = x_dbg
        g0["dbg_h"] = h_dbg
    g0["w1s"] = nc.dram_tensor("w1s", [NEX, 128, 8192], BF16)
    g0["w3s"] = nc.dram_tensor("w3s", [NEX, 128, 8192], BF16)
    g0["w2s"] = nc.dram_tensor("w2s", [NEX, 128, 8192], BF16)
    g0["xt_bounce"] = nc.dram_tensor("xt_bounce", [D, NT], BF16)
    g0["xt_gath"] = nc.dram_tensor("xt_gath", [2 * D, NT], BF16)
    g0["yacc"] = nc.dram_tensor("yacc", [D, T], BF16)
    g0["y_bounce"] = nc.dram_tensor("y_bounce", [2 * D, NT], BF16)
    g0["y_red"] = nc.dram_tensor("y_red", [D, NT], BF16)

    es = ExitStack()
    with es:
        def sb(name, shape, dt):
            g0[name] = es.enter_context(nc.sbuf_tensor(name, list(shape), dt))

        sb("x_res", [128, DC * NT], F32)
        sb("A", [128, ARENA], BF16)
        sb("scr", [128, 2048], F32)
        sb("smf", [128, 1024], F32)
        sb("identS", [128, 128], F32)
        sb("ones_bfS", [128, 128], BF16)
        sb("onesrow_fS", [1, 128], F32)
        sb("headonesS", [128, 8], BF16)
        sb("cos2S", [128, T], BF16)
        sb("sin2S", [128, T], BF16)
        sb("masksS", [128, 4 * TT], BF16)
        sb("ccolS", [128, 4], F32)
        sb("gwt_sb", [128, DC * E], F32)
        sb("gb_sb", [1, E], F32)
        sb("lng_sb", [128, LC], F32)
        sb("lnb_sb", [128, LC], F32)
        sb("gates_fm", [8, NT], BF16)
        sb("sel8S", [8, E * 128], BF16)
        sb("small2", [128, 48], F32)

        for nm, shape in [("pbig0", [128, 1024]), ("pbig1", [128, 1024]),
                          ("pbig2", [128, 1024]),
                          ("p6", [128, 512]), ("p7", [128, 512])]:
            g0[nm] = es.enter_context(nc.psum_tensor(nm, shape, F32))
        g0["p4"] = g0["pbig2"][:, 0:512]
        g0["p5"] = g0["pbig2"][:, 512:1024]

        sems = ["cst", "xld", "sw1", "sw3", "sw2", "tw1", "tw3", "tw2",
                "xta", "xtb", "ypa", "ypb", "yra", "yrb", "bnc", "bna",
                "bnb", "gw1", "gw3", "gw2", "gwa", "scd", "outw", "sl2",
                "mm", "vv", "ss", "cc", "gg"]
        semh = {n: es.enter_context(nc.semaphore(n)) for n in sems}
        blk = es.enter_context(nc.Block())

        def run(which):
            def f(e):
                program(Pass(which, e, semh), g0)
            return f

        blk.sync(run("sync"))
        blk.tensor(run("tensor"))
        blk.vector(run("vector"))
        blk.scalar(run("scalar"))
        blk.gpsimd(run("gpsimd"))

    return nc



def xr(g, c, t0, t1):
    return g["x_res"][:, c * NT + t0: c * NT + t1]


def av(g, off, n):
    return g["A"][:, off: off + n]


def avr(g, off, n, p0, p1):
    return g["A"][p0:p1, off: off + n]


def program(P, g0):
    g = dict(g0)
    g["_st"] = {}
    stage = g["stage"]

    emit_const_loads(P, g)
    emit_xload(P, g)

    emit_ffn_sublayer(P, g, app=0)
    if stage == "ffn1":
        emit_output(P, g)
        return
    emit_attention(P, g)
    if stage == "attn":
        emit_output(P, g)
        return
    nex2 = NEX
    if stage.startswith("full") and len(stage) > 4:
        nex2 = int(stage[4:])
    emit_ffn_sublayer(P, g, app=1, nex=nex2)
    emit_output(P, g)


def emit_const_loads(P, g):
    dr = g["dr"]
    for s, d in [("identS", "ident"), ("ones_bfS", "ones_bf"),
                 ("onesrow_fS", "onesrow_f"), ("headonesS", "headones"),
                 ("cos2S", "cos2"), ("sin2S", "sin2"), ("masksS", "masks"),
                 ("ccolS", "ccol"), ("gb_sb", "gbias"), ("lng_sb", "ln_g"),
                 ("lnb_sb", "ln_b"), ("sel8S", "sel8")]:
        P.dma("sync", g[s][:, :], dr[d][:, :], "cst")
    src = bass.AP(dr["gwt"], 0, [[E, 128], [128 * E, DC], [1, E]])
    P.dma("sync", g["gwt_sb"][:, :].rearrange("p (c e) -> p c e", c=DC), src, "cst")
    g["const_sld"] = P.snap("cst")


def emit_xload(P, g):
    src = bass.AP(g["dr"]["x_sh"], 0, [[NT, 128], [128 * NT, DC], [1, NT]])
    P.dma("sync", g["x_res"][:, :].rearrange("p (c t) -> p c t", c=DC), src, "xld")
    g["xload_sld"] = P.snap("xld")



def emit_ln(P, g, with_gate, bounce=None):
    A, scr, smf = g["A"], g["scr"], g["smf"]
    S_ps, Q_ps = g["p4"], g["p5"]
    st = g["_st"]
    P.wait("vector", "mm", P.snap("mm"))
    P.wait("scalar", "mm", P.snap("mm"))
    P.wait("vector", "gg", P.snap("gg"))
    P.wait("scalar", "gg", P.snap("gg"))
    P.wait("vector", "xld", P.snap("xld"))
    P.drain("vector")
    P.drain("scalar")
    ln_mm = {}
    for nt in range(NTT):
        for c in range(DC):
            xb = av(g, ALN + (c % 2) * 512, 512)
            xq = av(g, ALN + 1024 + (c % 2) * 512, 512)
            if c >= 2:
                P.wait("vector", "mm", ln_mm[c - 2])
            P.drain("vector")
            P.op("vector", "tensor_copy", sem="vv", out=xb,
                 in_=xr(g, c, nt * TT, (nt + 1) * TT))
            P.waitc("scalar", "vv")
            P.drain("scalar")
            P.op("scalar", "activation", sem="ss", out=xq, in_=xb, func=AF.Square)
            P.waitc("tensor", "vv")
            P.waitc("tensor", "ss")
            P.wait("tensor", "cst", g["const_sld"])
            P.op("tensor", "matmul", S_ps[:, :], g["ones_bfS"][:, :], xb,
                 start=(c == 0), stop=(c == DC - 1))
            ln_mm[c] = P.op("tensor", "matmul", Q_ps[:, :], g["ones_bfS"][:, :],
                            xq, start=(c == 0), stop=(c == DC - 1), sem="mm")
        stats_mm = P.snap("mm")
        mu2, var, xm, rstd = (scr[:, 0:512], scr[:, 512:1024],
                              scr[:, 1024:1536], scr[:, 1536:2048])
        P.wait("scalar", "mm", stats_mm)
        P.drain("scalar")
        P.op("scalar", "activation", sem="ss", out=mu2, in_=S_ps[:, :],
             func=AF.Square, scale=1.0 / D)
        P.waitc("vector", "ss")
        P.wait("vector", "mm", stats_mm)
        P.drain("vector")
        P.op("vector", "scalar_tensor_tensor", sem="vv", out=var, in0=Q_ps[:, :],
             scalar=1.0 / D, in1=mu2, op0=OP.mult, op1=OP.subtract)
        P.waitc("scalar", "vv")
        P.drain("scalar")
        P.op("scalar", "activation", sem="ss", out=rstd, in_=var, func=AF.Ln,
             bias=g["ccolS"][:, 0:1], scale=1.0)
        P.drain("scalar")
        P.op("scalar", "activation", sem="ss", out=rstd, in_=rstd, func=AF.Exp,
             scale=-0.5)
        recip_vv = P.snap("vv")
        rstd_ss = P.snap("ss")
        P.wait("vector", "ss", rstd_ss)
        for c in range(DC):
            P.drain("vector")
            P.op("vector", "scalar_tensor_tensor", out=xm, in0=S_ps[:, :],
                 scalar=-1.0 / D, in1=xr(g, c, nt * TT, (nt + 1) * TT),
                 op0=OP.mult, op1=OP.add)
            P.drain("vector")
            xtv = av(g, AXT + c * NT + nt * TT, TT)
            xt_vv = P.op("vector", "tensor_tensor", sem="vv", out=xtv, in0=xm,
                         in1=rstd, op=OP.mult)
            if with_gate:
                x32 = smf[:, (c % 2) * 512:(c % 2) * 512 + 512]
                if c >= 2:
                    P.wait("vector", "mm", st.get(("gmm", c - 2), 0))
                P.drain("vector")
                P.op("vector", "tensor_tensor", sem="vv", out=x32, in0=xm,
                     in1=rstd, op=OP.mult)
                P.waitc("tensor", "vv")
                gv = P.op("tensor", "matmul", g["p6"][0:E, :],
                          g["gwt_sb"][:, c * E:(c + 1) * E], x32,
                          start=(c == 0), stop=(c == DC - 1), sem="mm")
                st[("gmm", c)] = gv
            if bounce is not None:
                eng = "sync" if c % 2 == 0 else "scalar"
                q = "bnc" if c % 2 == 0 else "scd"
                P.wait(eng, "vv", xt_vv)
                dst = bass.AP(bounce, c * 128 * NT + nt * TT, [[NT, 128], [1, TT]])
                P.dma(eng, dst, xtv, q)
        if with_gate:
            emit_gate_math(P, g, nt)
    g["xt_vv"] = P.snap("vv")


def emit_gate_math(P, g, nt):
    scr, smf, small2 = g["scr"], g["smf"], g["small2"]
    G_ps, T_ps = g["p6"], g["p7"]
    st = g["_st"]
    gate_mm = P.snap("mm")
    gsc = scr[0:E, 0:512]
    P.wait("vector", "mm", gate_mm)
    P.drain("vector")
    P.op("vector", "tensor_copy", sem="vv", out=gsc, in_=G_ps[0:E, :])
    gevac = P.snap("vv")

    if not st.get("bias8"):
        st["bias8"] = True
        P.wait("tensor", "cst", g["const_sld"])
        b8v = P.op("tensor", "matmul", T_ps[:, 0:E], g["onesrow_fS"][:, :],
                   g["gb_sb"][:, :], start=True, stop=True, sem="mm")
        P.wait("vector", "mm", b8v)
        P.drain("vector")
        P.op("vector", "tensor_copy", sem="vv", out=small2[:, 32:40],
             in_=T_ps[:, 0:E])
        st["bias8_vv"] = P.snap("vv")
    bias8 = small2[:, 32:40]

    for sub in range(4):
        gt = T_ps[:, 8 + sub * 8: 16 + sub * 8]
        P.wait("tensor", "vv", gevac)
        trv = P.op("tensor", "transpose", gt, gsc[:, sub * 128:(sub + 1) * 128],
                   g["identS"][0:E, 0:E], sem="mm")
        nmax, sum8, rec = (smf[:, 0:1], smf[:, 1:2], smf[:, 2:3])
        e8, b8, m8, msk, gtm = (small2[:, 0:8], small2[:, 8:16],
                                small2[:, 16:24], small2[:, 24:32],
                                small2[:, 40:48])
        P.wait("vector", "mm", trv)
        P.drain("vector")
        P.op("vector", "reduce_max", sem="vv", out=nmax, in_=gt, axis=AXX,
             negate=True)
        P.waitc("scalar", "vv")
        P.drain("scalar")
        P.op("scalar", "activation", sem="ss", out=e8, in_=gt, func=AF.Exp,
             bias=nmax, accum_out=sum8)
        P.waitc("vector", "ss")
        P.drain("vector")
        P.op("vector", "reciprocal", out=rec, in_=sum8)
        P.drain("vector")
        P.op("vector", "tensor_scalar", out=e8, in0=e8, scalar1=rec,
             scalar2=None, op0=OP.mult)
        P.drain("vector")
        P.wait("vector", "vv", st["bias8_vv"])
        P.op("vector", "tensor_tensor", out=b8, in0=e8, in1=bias8, op=OP.add)
        P.drain("vector")
        P.op("vector", "max", out=m8, in_=b8)
        P.drain("vector")
        P.op("vector", "tensor_scalar", out=msk, in0=b8, scalar1=m8[:, 1:2],
             scalar2=None, op0=OP.is_ge)
        P.drain("vector")
        P.op("vector", "tensor_tensor", sem="vv", out=gtm, in0=e8, in1=msk,
             op=OP.mult)
        P.waitc("tensor", "vv")
        tb = P.op("tensor", "transpose", G_ps[0:8, sub * 128:(sub + 1) * 128],
                  gtm, g["identS"][:, :], sem="mm")
        P.wait("vector", "mm", tb)
        P.drain("vector")
        P.op("vector", "tensor_copy", sem="vv",
             out=g["gates_fm"][0:8, nt * TT + sub * 128: nt * TT + (sub + 1) * 128],
             in_=G_ps[0:8, sub * 128:(sub + 1) * 128])
    g["gates_vv"] = P.snap("vv")



def wsrc_ap(g, kind, e, idx):
    dr = g["dr"]
    if kind in ("w1", "w3"):
        if e < E:
            return bass.AP(dr[kind + "t"], e * D * D + idx * 512,
                           [[D, 128], [128 * D, DC], [1, 512]])
        return bass.AP(dr["s" + kind + "t"], (e - E) * 1024 + idx * 512,
                       [[ISH, 128], [128 * ISH, DC], [1, 512]])
    if e < E:
        return bass.AP(dr["w2t"], e * D * D + idx * 256,
                       [[D, 128], [128 * D, DC], [1, 256]])
    return bass.AP(dr["sw2t"], (e - E) * 8 * 128 * D + idx * 256,
                   [[D, 128], [128 * D, DC], [1, 256]])


def stash_ap(g, kind, e, idx):
    n = 4096 if kind in ("w1", "w3") else 2048
    return g[kind + "s"][e, :, idx * n:(idx + 1) * n]


def wslot(g, kind, slot):
    if kind == "w1":
        return av(g, AW1 + slot * 4096, 4096)
    if kind == "w3":
        return av(g, AW3 + slot * 4096, 4096)
    return av(g, AW2 + slot * 2048, 2048)


def emit_wload(P, g, app, kind, e, idx, slot, wait_mm=None, wait_vv=None):
    sv = wslot(g, kind, slot)
    gsem = {"w1": "gw1", "w3": "gw3", "w2": "gw2"}[kind]
    ssem = {"w1": "sw1", "w3": "sw3", "w2": "sw2"}[kind]
    tsem = {"w1": "tw1", "w3": "tw3", "w2": "tw2"}[kind]
    st = g["_st"]
    if app == 0:
        P.wait("gpsimd", "mm", wait_mm)
        P.wait("gpsimd", "vv", wait_vv)
        P.wait("gpsimd", tsem, st.get(("stw", kind, slot), 0))
        P.waitc("gpsimd", gsem)
        dsh = sv.rearrange("p (a b) -> p a b", a=DC)
        P.dma("gpsimd", dsh, wsrc_ap(g, kind, e, idx), gsem)
        v = P.snap(gsem)
        P.wait("sync", gsem, v)
        P.waitc("sync", tsem)
        P.dma("sync", stash_ap(g, kind, e, idx), sv, tsem)
        st[("stw", kind, slot)] = P.snap(tsem)
        return (gsem, v)
    P.wait("sync", "mm", wait_mm)
    P.wait("sync", "vv", wait_vv)
    P.waitc("sync", ssem)
    P.dma("sync", sv, stash_ap(g, kind, e, idx), ssem)
    return (ssem, P.snap(ssem))


def psum_fence(P):
    P.wait("tensor", "ss", P.snap("ss"))
    P.wait("tensor", "vv", P.snap("vv"))


def emit_ffn_sublayer(P, g, app, nex=NEX):
    emit_ln(P, g, with_gate=True)
    st = g["_st"]
    scr = g["scr"]
    hpairs = [(g["pbig0"][:, 0:512], g["pbig0"][:, 512:1024]),
              (g["pbig1"][:, 0:512], g["pbig1"][:, 512:1024]),
              (g["p4"][:, :], g["p5"][:, :])]
    Gb = g["p6"]
    yb = [g["p4"], g["p5"], g["p7"]]
    ones1_bf = g["ones_bfS"][0:1, :]
    cvring = []
    yring = []
    silu_ring = [scr[:, 0:256].bitcast(BF16), scr[:, 256:512].bitcast(BF16),
                 scr[:, 512:768].bitcast(BF16)]
    tmp_ring = [scr[:, 768:1024].bitcast(BF16), scr[:, 1024:1280].bitcast(BF16)]
    uc = [0]

    psum_fence(P)
    first_mm = P.snap("mm")
    first_vv = P.snap("vv")
    ld = {}
    if app == 1 and "preld" in g:
        ld.update(g.pop("preld"))
    else:
        for kind, idx, slot in [("w1", 0, 0), ("w3", 0, 0), ("w2", 0, 0),
                                ("w1", 1, 1), ("w3", 1, 1), ("w2", 1, 1)]:
            ld[(0, kind, idx)] = emit_wload(P, g, app, kind, 0, idx, slot,
                                            wait_mm=first_mm, wait_vv=first_vv)

    for ei in range(nex):
        is_sh = ei >= E
        psum_fence(P)
        for nt in range(NTT):
            if not is_sh:
                P.wait("tensor", "vv", g["gates_vv"])
                P.wait("tensor", "gg", st.get("gbc_used", 0))
                P.wait("tensor", "ss", st.get("gbc_evac", 0))
                gbv = P.op("tensor", "matmul", Gb[:, :],
                           g["sel8S"][:, ei * 128:(ei + 1) * 128],
                           g["gates_fm"][0:8, nt * TT:(nt + 1) * TT],
                           start=True, stop=True, sem="mm")
                st["gbc_mm"] = gbv
                P.wait("scalar", "mm", gbv)
                P.wait("scalar", "gg", st.get("gbc_used", 0))
                P.drain("scalar")
                st["gbc_evac"] = P.op("scalar", "copy", sem="ss",
                                      out=g["smf"][:, 0:512], in_=Gb[:, :])
            for ic in range(DC):
                u = uc[0]; uc[0] += 1
                hb, h3 = hpairs[u % 3]
                hf, icl = ic // 4, ic % 4
                w1v, w3v = wslot(g, "w1", hf), wslot(g, "w3", hf)
                s1, v1 = ld[(ei, "w1", hf)]
                s3, v3 = ld[(ei, "w3", hf)]
                P.wait("tensor", s1, v1)
                P.wait("tensor", s3, v3)
                if len(cvring) >= 3:
                    P.wait("tensor", "vv", cvring[-3][0])
                    P.wait("tensor", "ss", cvring[-3][2])
                for kc in range(DC):
                    P.op("tensor", "matmul", hb,
                         w1v[:, kc * 512 + icl * 128: kc * 512 + (icl + 1) * 128],
                         av(g, AXT + kc * NT + nt * TT, TT),
                         start=(kc == 0), stop=(kc == DC - 1))
                mv = None
                for kc in range(DC):
                    mv = P.op("tensor", "matmul", h3,
                              w3v[:, kc * 512 + icl * 128: kc * 512 + (icl + 1) * 128],
                              av(g, AXT + kc * NT + nt * TT, TT),
                              start=(kc == 0), stop=(kc == DC - 1),
                              sem="mm" if kc == DC - 1 else None)
                silu = silu_ring[u % 3]
                P.wait("scalar", "mm", mv)
                P.drain("scalar")
                sse = P.op("scalar", "activation", sem="ss", out=silu, in_=hb,
                           func=AF.Silu)
                hout = av(g, AH + ic * NT + nt * TT, TT)
                P.waitc("vector", "ss")
                if not is_sh:
                    last2 = (nt == NTT - 1 and ic >= DC - 2)
                    tmp = tmp_ring[u % 2]
                    if len(cvring) >= 2 and cvring[-2][1]:
                        P.wait("vector", "gg", cvring[-2][1])
                    P.drain("vector")
                    tv = P.op("vector", "tensor_tensor", sem="vv", out=tmp,
                              in0=h3, in1=silu, op=OP.mult)
                    if last2:
                        P.wait("vector", "ss", st["gbc_evac"])
                        if len(cvring) >= 2 and cvring[-2][1]:
                            P.wait("vector", "gg", cvring[-2][1])
                        P.drain("vector")
                        cv = P.op("vector", "tensor_tensor", sem="vv",
                                  out=hout, in0=tmp, in1=g["smf"][:, 0:512],
                                  op=OP.mult)
                        cvring.append((cv, None, sse))
                    else:
                        P.wait("gpsimd", "vv", tv)
                        P.wait("gpsimd", "ss", st["gbc_evac"])
                        if len(cvring) >= 2:
                            P.wait("gpsimd", "gg", cvring[-2][1])
                        cv = P.op("gpsimd", "tensor_tensor", sem="gg", out=hout,
                                  in0=tmp, in1=g["smf"][:, 0:512], op=OP.mult)
                        cvring.append((tv, cv, sse))
                else:
                    P.drain("vector")
                    cv = P.op("vector", "tensor_tensor", sem="vv", out=hout,
                              in0=h3, in1=silu, op=OP.mult)
                    cvring.append((cv, None, sse))
            st["gbc_used"] = P.snap("gg")
        a_done_mm = P.snap("mm")
        h_done_vv = P.snap("vv")
        h_done_gg = P.snap("gg")
        if ei + 1 < nex:
            for kind in ("w1", "w3"):
                for hf in range(2):
                    ld[(ei + 1, kind, hf)] = emit_wload(
                        P, g, app, kind, ei + 1, hf, hf, wait_mm=a_done_mm)
        psum_fence(P)
        P.wait("tensor", "gg", P.snap("gg"))
        for dcx in range(DC):
            qv = dcx // 2
            s2, v2 = ld[(ei, "w2", qv)]
            for nt in range(NTT):
                ybank = yb[(dcx * NTT + nt) % 3]
                P.wait("tensor", s2, v2)
                P.wait("tensor", "vv", h_done_vv)
                P.wait("tensor", "gg", h_done_gg)
                if len(yring) >= 3:
                    P.wait("tensor", "vv", yring[-3])
                mv = None
                for icx in range(DC):
                    mv = P.op("tensor", "matmul", ybank[:, :],
                              wslot(g, "w2", qv % 2)[:, icx * 256 + (dcx % 2) * 128:
                                                     icx * 256 + (dcx % 2) * 128 + 128],
                              av(g, AH + icx * NT + nt * TT, TT),
                              start=(icx == 0), stop=(icx == DC - 1),
                              sem="mm" if icx == DC - 1 else None)
                P.wait("vector", "mm", mv)
                P.drain("vector")
                yv = P.op("vector", "scalar_tensor_tensor", sem="vv",
                          out=xr(g, dcx, nt * TT, (nt + 1) * TT), in0=ybank[:, :],
                          scalar=0.5, in1=xr(g, dcx, nt * TT, (nt + 1) * TT),
                          op0=OP.mult, op1=OP.add)
                yring.append(yv)
            if dcx % 2 == 1 and dcx < 4:
                nq = dcx // 2 + 2
                ld[(ei, "w2", nq)] = emit_wload(P, g, app, "w2", ei, nq,
                                                nq % 2, wait_mm=P.snap("mm"))
            if dcx == DC - 1 and ei + 1 < nex:
                for q2 in range(2):
                    ld[(ei + 1, "w2", q2)] = emit_wload(
                        P, g, app, "w2", ei + 1, q2, q2, wait_mm=P.snap("mm"))
    g["xres_writer_vv"] = P.snap("vv")



def emit_attention(P, g):
    emit_ln(P, g, with_gate=False, bounce=g["xt_bounce"])
    st = g["_st"]
    scr, smf = g["scr"], g["smf"]

    P.waitc("gpsimd", "bnc")
    P.waitc("gpsimd", "scd")
    P.waitc("sync", "tw1")
    P.waitc("sync", "tw3")
    P.waitc("sync", "tw2")
    P.waitc("sync", "gw1")
    P.waitc("sync", "gw3")
    P.waitc("sync", "gw2")
    P.op("gpsimd", "collective_compute", "AllGather", OP.bypass,
         replica_groups=RG, ins=[g["xt_bounce"][:, :].opt()],
         outs=[g["xt_gath"][:, :].opt()], sem="cc")
    g["ag_cc"] = P.snap("cc")
    if not g["_st"].get("sel2_loaded"):
        g["_st"]["sel2_loaded"] = True
        P.dma("sync", avr(g, ASP + 1536, 256, 0, 33).bitcast(F32),
              g["dr"]["sel2"][:, :], "sl2")
        g["sel2_v"] = P.snap("sl2")

    P.wait("gpsimd", "mm", P.snap("mm"))
    P.waitc("gpsimd", "tw1")
    P.waitc("gpsimd", "tw3")
    P.waitc("gpsimd", "tw2")
    wk = av(g, AWS, 4096)
    src = bass.AP(g["dr"]["wdkv"], 0, [[LAT, 128], [128 * LAT, DC], [1, LAT]])
    P.dma("gpsimd", wk[:, 0:DC * LAT].rearrange("p (a b) -> p a b", a=DC),
          src, "gwa")
    wdkv_gld = P.snap("gwa")

    lat_ps = [g["p4"], g["p5"]]
    psum_fence(P)
    xt_used = [0, 0]
    W2 = 512
    for ntg in range(NQG):
        half = ntg // 4
        xti = av(g, AO + (ntg % 2) * 4096, 4096)
        xsem = "xta" if ntg % 2 == 0 else "xtb"
        P.wait("sync", "cc", g["ag_cc"])
        P.wait("sync", "mm", xt_used[ntg % 2])
        src = bass.AP(g["xt_gath"], half * D * NT + (ntg % 4) * W2,
                      [[NT, 128], [128 * NT, DC], [1, W2]])
        P.dma("sync", xti.rearrange("p (c j) -> p c j", c=DC), src, xsem)
        xt_sld = P.snap(xsem)
        for lc in range(LC):
            P.wait("tensor", xsem, xt_sld)
            P.wait("tensor", "gwa", wdkv_gld)
            P.wait("tensor", "vv", st.get(("lat_evac", ntg - 1, lc), 0))
            for c in range(DC):
                P.op("tensor", "matmul", lat_ps[lc][:, 0:W2],
                     wk[:, c * LAT + lc * 128: c * LAT + (lc + 1) * 128],
                     xti[:, c * W2:(c + 1) * W2],
                     start=(c == 0), stop=(c == DC - 1),
                     sem="mm" if c == DC - 1 else None)
            st[("lat_mm", ntg, lc)] = P.snap("mm")
        xt_used[ntg % 2] = P.snap("mm")
        emit_lat_ln(P, g, ntg, lat_ps, W2)
    st[("xti_used", 0)] = xt_used[0]
    st[("xti_used", 1)] = xt_used[1]
    g["lat_vv"] = P.snap("vv")
    st["wsl_free_mm"] = P.snap("mm")

    for p in range(2):
        emit_attn_pass(P, g, p)

    if g["stage"] == "full":
        pmm, pvv = P.snap("mm"), P.snap("vv")
        preld = {}
        for kind, idx, slot in [("w1", 0, 0), ("w3", 0, 0), ("w2", 0, 0),
                                ("w1", 1, 1), ("w3", 1, 1), ("w2", 1, 1)]:
            preld[(0, kind, idx)] = emit_wload(P, g, 1, kind, 0, idx, slot,
                                               wait_mm=pmm, wait_vv=pvv)
        g["preld"] = preld

    P.waitc("gpsimd", "bna")
    P.waitc("gpsimd", "bnb")
    P.op("gpsimd", "collective_compute", "ReduceScatter", OP.add,
         replica_groups=RG, ins=[g["y_bounce"][:, :].opt()],
         outs=[g["y_red"][:, :].opt()], sem="cc")
    rs_cc = P.snap("cc")
    ytA = av(g, ALAT, 512)
    ytB = av(g, ALAT + 512, 512)
    for dcx in range(DC):
        for nt in range(NTT):
            par = (dcx * NTT + nt) % 2
            yt = ytA if par == 0 else ytB
            ysem = "yra" if par == 0 else "yrb"
            P.wait("sync", "cc", rs_cc)
            P.wait("sync", "vv", st.get(("yred_used", par), 0))
            src = bass.AP(g["y_red"], dcx * 128 * NT + nt * TT,
                          [[NT, 128], [1, TT]])
            P.dma("sync", yt, src, ysem)
            ysld = P.snap(ysem)
            P.wait("vector", ysem, ysld)
            P.drain("vector")
            yv = P.op("vector", "tensor_tensor", sem="vv",
                      out=xr(g, dcx, nt * TT, (nt + 1) * TT),
                      in0=xr(g, dcx, nt * TT, (nt + 1) * TT), in1=yt,
                      op=OP.add)
            st[("yred_used", (dcx * NTT + nt) % 2)] = yv
    g["xres_writer_vv"] = P.snap("vv")


def emit_lat_ln(P, g, ntg, lat_ps, W):
    st = g["_st"]
    scr = g["scr"]
    lraw = [av(g, ASP, W), av(g, ASP + W, W)]
    sq = av(g, ASP + 2 * W, W)
    S2, Q2 = g["p6"], g["p7"]
    for lc in range(LC):
        P.wait("vector", "mm", st[("lat_mm", ntg, lc)])
        P.drain("vector")
        P.op("vector", "tensor_copy", sem="vv", out=lraw[lc],
             in_=lat_ps[lc][:, 0:W])
    lraw_vv = P.snap("vv")
    sqv = {}
    for lc in range(LC):
        P.wait("scalar", "vv", lraw_vv)
        if lc == 1:
            P.wait("scalar", "mm", sqv[0])
        P.drain("scalar")
        P.op("scalar", "activation", sem="ss", out=sq, in_=lraw[lc],
             func=AF.Square)
        sq_ss = P.snap("ss")
        P.wait("tensor", "vv", lraw_vv)
        P.wait("tensor", "ss", sq_ss)
        if lc == 0:
            P.wait("tensor", "vv", st.get("latln_prev_vv", 0))
        P.op("tensor", "matmul", S2[:, 0:W], g["ones_bfS"][:, :], lraw[lc],
             start=(lc == 0), stop=(lc == LC - 1))
        sqv[lc] = P.op("tensor", "matmul", Q2[:, 0:W], g["ones_bfS"][:, :], sq,
                       start=(lc == 0), stop=(lc == LC - 1), sem="mm")
    smm = P.snap("mm")
    mu2, var, xm, rstd = (scr[:, 0:W], scr[:, 512:512 + W],
                          scr[:, 1024:1024 + W], scr[:, 1536:1536 + W])
    P.wait("scalar", "mm", smm)
    P.drain("scalar")
    P.op("scalar", "activation", sem="ss", out=mu2, in_=S2[:, 0:W],
         func=AF.Square, scale=1.0 / LAT)
    P.waitc("vector", "ss")
    P.wait("vector", "mm", smm)
    P.drain("vector")
    P.op("vector", "scalar_tensor_tensor", sem="vv", out=var, in0=Q2[:, 0:W],
         scalar=1.0 / LAT, in1=mu2, op0=OP.mult, op1=OP.subtract)
    P.waitc("scalar", "vv")
    P.drain("scalar")
    P.op("scalar", "activation", sem="ss", out=rstd, in_=var, func=AF.Ln,
         bias=g["ccolS"][:, 0:1], scale=1.0)
    P.drain("scalar")
    P.op("scalar", "activation", sem="ss", out=rstd, in_=rstd, func=AF.Exp,
         scale=-0.5)
    P.waitc("vector", "ss")
    for lc in range(LC):
        P.drain("vector")
        P.op("vector", "scalar_tensor_tensor", out=xm, in0=S2[:, 0:W],
             scalar=-1.0 / LAT, in1=lraw[lc], op0=OP.mult, op1=OP.add)
        P.drain("vector")
        P.op("vector", "tensor_tensor", out=xm, in0=xm, in1=rstd, op=OP.mult)
        P.drain("vector")
        ev = P.op("vector", "tensor_scalar", sem="vv",
                  out=av(g, ALAT + lc * T + ntg * W, W), in0=xm,
                  scalar1=g["lng_sb"][:, lc:lc + 1],
                  scalar2=g["lnb_sb"][:, lc:lc + 1], op0=OP.mult, op1=OP.add)
        st[("lat_evac", ntg, lc)] = ev
    st["latln_prev_vv"] = P.snap("vv")


def emit_rope(P, g, src_ps, mv, dst_bf, tcol0, W):
    scr = g["scr"]
    qsw = scr[:, 0:W]
    t1 = scr[:, 512:512 + W]
    mask = [(i ^ 1) for i in range(32)]
    P.wait("vector", "mm", mv)
    P.drain("vector")
    P.op("vector", "stream_shuffle", out=qsw, in_=src_ps[:, 0:W], mask=mask)
    P.op("vector", "tensor_tensor", out=t1, in0=src_ps[:, 0:W],
         in1=g["cos2S"][:, tcol0:tcol0 + W], op=OP.mult)
    P.drain("vector")
    P.op("vector", "tensor_tensor", out=qsw, in0=qsw,
         in1=g["sin2S"][:, tcol0:tcol0 + W], op=OP.mult)
    P.drain("vector")
    return P.op("vector", "tensor_tensor", sem="vv", out=dst_bf, in0=t1,
                in1=qsw, op=OP.add)


def emit_norm_acc(P, g, which, ntg, acc, abase, col0, W):
    st = g["_st"]
    smf = g["smf"]
    sqs = av(g, ASP, W)
    N_ps = g["p6"]
    mv = None
    for qc in range(2):
        P.wait("scalar", "vv", st[(which + "_done", ntg, qc)])
        P.wait("scalar", "mm", st.get(("nq_mm", which), 0))
        P.drain("scalar")
        P.op("scalar", "activation", sem="ss", out=sqs,
             in_=av(g, abase + qc * T + col0, W), func=AF.Square)
        sq_ss = P.snap("ss")
        P.wait("tensor", "ss", sq_ss)
        if qc == 0:
            P.wait("tensor", "vv", st.get(("nred", which, ntg - 1), 0))
        mv = P.op("tensor", "matmul", N_ps[0:4, 0:W],
                  g["headonesS"][:, qc * 4:(qc + 1) * 4], sqs,
                  start=(qc == 0), stop=(qc == 1), sem="mm")
        st[("nq_mm", which)] = mv
    ntmax = smf[0:4, 20:21]
    P.wait("vector", "mm", mv)
    P.drain("vector")
    P.op("vector", "reduce_max", out=ntmax, in_=N_ps[0:4, 0:W], axis=AXX)
    P.drain("vector")
    if ntg == 0:
        rv = P.op("vector", "tensor_copy", sem="vv", out=acc, in_=ntmax)
    else:
        rv = P.op("vector", "tensor_tensor", sem="vv", out=acc, in0=acc,
                  in1=ntmax, op=OP.max)
    st[("nred", which, ntg)] = rv


def emit_bias(P, g, qmax, kmax):
    st = g["_st"]
    smf = g["smf"]
    t = smf[0:4, 21:22]
    P.drain("vector")
    tv = P.op("vector", "tensor_tensor", sem="vv", out=t, in0=qmax, in1=kmax,
              op=OP.mult)
    P.wait("scalar", "vv", tv)
    P.drain("scalar")
    sv = P.op("scalar", "activation", sem="ss", out=t, in_=t, func=AF.Sqrt)
    P.wait("vector", "ss", sv)
    P.drain("vector")
    bv = P.op("vector", "tensor_scalar", sem="vv", out=t, in0=t,
              scalar1=-0.125, scalar2=0.25, op0=OP.mult, op1=OP.subtract)
    P.wait("tensor", "vv", bv)
    P.wait("tensor", "vv", st["v_done"])
    tr = P.op("tensor", "transpose", g["p7"][0:1, 0:4], t,
              g["identS"][0:4, 0:4], sem="mm")
    P.wait("vector", "mm", tr)
    P.drain("vector")
    rv = P.op("vector", "tensor_copy", sem="vv", out=smf[0:1, 24:28],
              in_=g["p7"][0:1, 0:4])
    P.wait("tensor", "vv", rv)
    bb = P.op("tensor", "matmul", g["p7"][:, 8:12], g["onesrow_fS"][:, :],
              smf[0:1, 24:28], start=True, stop=True, sem="mm")
    P.wait("vector", "mm", bb)
    P.drain("vector")
    P.op("vector", "tensor_copy", sem="vv", out=smf[:, 16:20],
         in_=g["p7"][:, 8:12])
    st["bias_vv"] = P.snap("vv")
    P.wait("scalar", "vv", st["bias_vv"])


def emit_attn_pass(P, g, p):
    st = g["_st"]
    scr, smf = g["scr"], g["smf"]
    wsl = av(g, AWS, 4096)
    qmax = smf[0:4, 8:9]
    kmax = smf[0:4, 9:10]

    P.wait("gpsimd", "mm", st["wsl_free_mm"])
    src = bass.AP(g["dr"]["wq_sh"], 0,
                  [[HL * DH, 128], [128 * HL * DH, DC], [1, HL * DH]])
    P.dma("gpsimd", wsl.rearrange("p (a b) -> p a b", a=DC), src, "gwa")
    wq_gld = P.snap("gwa")

    q_ps = [g["p4"], g["p5"]]
    psum_fence(P)
    st[("xti_used", 0)] = max(st.get(("xti_used", 0), 0), st["wsl_free_mm"])
    st[("xti_used", 1)] = max(st.get(("xti_used", 1), 0), st["wsl_free_mm"])
    W2 = 512
    for ntg in range(NQG):
        half = ntg // 4
        xti = av(g, AO + (ntg % 2) * 4096, 4096)
        xsem = "xta" if ntg % 2 == 0 else "xtb"
        P.wait("sync", "cc", g["ag_cc"])
        P.wait("sync", "mm", st.get(("xti_used", ntg % 2), 0))
        src = bass.AP(g["xt_gath"], half * D * NT + (ntg % 4) * W2,
                      [[NT, 128], [128 * NT, DC], [1, W2]])
        P.dma("sync", xti.rearrange("p (c j) -> p c j", c=DC), src, xsem)
        xt_sld = P.snap(xsem)
        for qc in range(2):
            P.wait("tensor", xsem, xt_sld)
            P.wait("tensor", "gwa", wq_gld)
            P.wait("tensor", "vv", st.get(("q_done", ntg - 1, qc), 0))
            mv = None
            for c in range(DC):
                mv = P.op("tensor", "matmul", q_ps[qc][:, 0:W2],
                          wsl[:, c * 512 + (p * 2 + qc) * 128:
                              c * 512 + (p * 2 + qc) * 128 + 128],
                          xti[:, c * W2:(c + 1) * W2],
                          start=(c == 0), stop=(c == DC - 1),
                          sem="mm" if c == DC - 1 else None)
            st[("q_done", ntg, qc)] = emit_rope(
                P, g, q_ps[qc], mv, av(g, AQ + qc * T + ntg * W2, W2),
                ntg * W2, W2)
        st[("xti_used", ntg % 2)] = P.snap("mm")
        emit_norm_acc(P, g, "q", ntg, qmax, AQ, ntg * W2, W2)
    q_all_mm = P.snap("mm")

    P.wait("gpsimd", "mm", q_all_mm)
    src = bass.AP(g["dr"]["wuk_sh"], 0,
                  [[HL * DH, 128], [128 * HL * DH, LC], [1, HL * DH]])
    P.dma("gpsimd", wsl[:, 0:LC * HL * DH].rearrange("p (a b) -> p a b", a=LC),
          src, "gwa")
    wuk_gld = P.snap("gwa")
    for ntg in range(NQG):
        for kc in range(2):
            P.wait("tensor", "gwa", wuk_gld)
            P.wait("tensor", "vv", g["lat_vv"])
            if ntg == 0:
                P.wait("tensor", "vv", st[("q_done", NQG - 1, kc)])
            P.wait("tensor", "vv", st.get(("k_done", ntg - 1, kc), 0))
            mv = None
            for lc in range(LC):
                mv = P.op("tensor", "matmul", q_ps[kc][:, :],
                          wsl[:, lc * 512 + (p * 2 + kc) * 128:
                              lc * 512 + (p * 2 + kc) * 128 + 128],
                          av(g, ALAT + lc * T + ntg * TT, TT),
                          start=(lc == 0), stop=(lc == LC - 1),
                          sem="mm" if lc == LC - 1 else None)
            st[("k_done", ntg, kc)] = emit_rope(
                P, g, q_ps[kc], mv, av(g, AK + kc * T + ntg * TT, TT),
                ntg * TT, TT)
        emit_norm_acc(P, g, "k", ntg, kmax, AK, ntg * TT, TT)
    k_all_mm = P.snap("mm")

    P.wait("gpsimd", "mm", k_all_mm)
    src = bass.AP(g["dr"]["wuv_sh"], 0,
                  [[HL * DH, 128], [128 * HL * DH, LC], [1, HL * DH]])
    P.dma("gpsimd", wsl[:, 0:LC * HL * DH].rearrange("p (a b) -> p a b", a=LC),
          src, "gwa")
    wuv_gld = P.snap("gwa")
    P.wait("vector", "vv", st.get("v_used", 0))
    P.drain("vector")
    ones_vv = P.op("vector", "memset",
                   av(g, AV, 32 * 260).rearrange("p (kt h j) -> p kt h j",
                                                 kt=32, h=4)[:, :, :, 64:65],
                   1.0, sem="vv")
    vb2 = [g["p7"], g["p6"]]
    psum_fence(P)
    for kt in range(NKT):
        vb = vb2[kt % 2]
        P.wait("tensor", "gwa", wuv_gld)
        P.wait("tensor", "vv", g["lat_vv"])
        if kt < 2:
            P.wait("tensor", "vv", st.get(("nred", "k", NQG - 1), 0))
        P.wait("tensor", "vv", st.get(("v_evac", kt - 2), 0))
        mv = None
        for lc in range(LC):
            mv = P.op("tensor", "matmul", vb[:, 0:256],
                      av(g, ALAT + lc * T + kt * 128, 128),
                      wsl[:, lc * 512 + p * 256: lc * 512 + (p + 1) * 256],
                      start=(lc == 0), stop=(lc == LC - 1),
                      sem="mm" if lc == LC - 1 else None)
        P.wait("vector", "mm", mv)
        P.wait("vector", "vv", ones_vv)
        P.drain("vector")
        st[("v_evac", kt)] = P.op(
            "vector", "tensor_copy", sem="vv",
            out=av(g, AV + kt * 260, 260).rearrange("p (h j) -> p h j",
                                                    h=4)[:, :, 0:64],
            in_=vb[:, 0:256].rearrange("p (h j) -> p h j", h=4))
    st["v_done"] = P.snap("vv")

    emit_bias(P, g, qmax, kmax)

    P.waitc("scalar", "bna")
    P.waitc("scalar", "bnb")
    P.waitc("scalar", "ypa")
    P.waitc("scalar", "ypb")
    sgrp = [g["pbig0"], g["pbig1"], g["pbig2"]]
    obA, obB = g["p6"], g["p7"]
    psum_fence(P)
    if not st.get("lrinit"):
        st["lrinit"] = True
        P.drain("vector")
        P.op("vector", "memset", smf[0:33, 512:1024], 0.0)
    for pr in range(2):
        P.drain("vector")
        P.op("vector", "tensor_tensor", sem="vv", out=smf[:, 22 + pr:23 + pr],
             in0=smf[:, 16 + 2 * pr:17 + 2 * pr],
             in1=smf[:, 17 + 2 * pr:18 + 2 * pr], op=OP.min)
    pb_vv = P.snap("vv")
    P.wait("scalar", "vv", pb_vv)
    def norm_part2(P, g, pend):
        prq, qgq, lrss = pend
        P.wait("tensor", "ss", lrss)
        P.wait("tensor", "sl2", g["sel2_v"])
        P.wait("tensor", "vv", st.get("lbev_vv", 0))
        lb = P.op("tensor", "matmul", g["pbig2"][:, 0:512],
                  avr(g, ASP + 1536, 256, 0, 33).bitcast(F32),
                  smf[0:33, 512:1024], start=True, stop=True, sem="mm")
        lbev = scr[:, 1024:1536]
        P.wait("vector", "mm", lb)
        P.drain("vector")
        P.op("vector", "tensor_copy", sem="vv", out=lbev,
             in_=g["pbig2"][:, 0:512])
        P.drain("vector")
        P.op("vector", "tensor_tensor", sem="vv",
             out=avr(g, AO + prq * T + qgq * TT, TT, 0, 64),
             in0=scr[0:64, 0:512], in1=lbev[0:64, :], op=OP.mult)
        P.op("vector", "tensor_tensor", sem="vv",
             out=avr(g, AO + prq * T + qgq * TT, TT, 64, 128),
             in0=scr[64:128, 512:1024], in1=lbev[64:128, :], op=OP.mult)
        st["lbev_vv"] = P.snap("vv")

    for pr in range(2):
        hA, hB = 2 * pr, 2 * pr + 1
        for qg in range(NQG):
            nkt = 4 * (qg + 1)
            for step in range(nkt + 2):
                if step >= 2:
                    kt = step - 2
                    m0 = max(0, kt - 4 * qg)
                    ptv = av(g, APT + (kt % 2) * 1024, 1024)
                    rsem, rval = st[("pt_ready", kt)]
                    P.wait("tensor", rsem, rval)
                    if kt == 0:
                        P.wait("tensor", "vv", st.get("o_free", 0))
                        P.wait("tensor", "vv", st.get("lbev_vv", 0))
                    P.op("tensor", "matmul", obA[0:65, m0 * 128:512],
                         av(g, AV + kt * 260 + hA * 65, 65),
                         ptv[:, m0 * 128:512],
                         start=(kt == 0), stop=(kt == nkt - 1))
                    avmm = P.op("tensor", "matmul", obB[0:65, m0 * 128:512],
                                av(g, AV + kt * 260 + hB * 65, 65),
                                ptv[:, 512 + m0 * 128:1024],
                                start=(kt == 0), stop=(kt == nkt - 1), sem="mm")
                    st[("pt_av", kt % 2)] = avmm
                if step == 2 and st.get("pending_norm"):
                    norm_part2(P, g, st.pop("pending_norm"))
                if step < nkt:
                    kt = step
                    gi = kt % 3
                    grp = sgrp[gi]
                    m0 = max(0, kt - 4 * qg)
                    P.wait("tensor", "ss", st.get(("exp_ss", gi), 0))
                    if kt == 0:
                        P.wait("tensor", "vv", st["v_done"])
                    if gi == 2:
                        P.wait("tensor", "vv", st.get("lbev_vv", 0))
                    P.op("tensor", "matmul", grp[:, 0:512],
                         avr(g, AK + pr * T + kt * 128, 128, 0, 64),
                         avr(g, AQ + pr * T + qg * TT, TT, 0, 64),
                         start=True, stop=True, tile_position=(0, 0))
                    smm = P.op("tensor", "matmul", grp[:, 512:1024],
                               avr(g, AK + pr * T + kt * 128, 128, 64, 128),
                               avr(g, AQ + pr * T + qg * TT, TT, 64, 128),
                               start=True, stop=True, sem="mm",
                               tile_position=(64, 0))
                    ptv = av(g, APT + (kt % 2) * 1024, 1024)
                    P.wait("scalar", "mm", smm)
                    P.wait("scalar", "mm", st.get(("pt_av", kt % 2), 0))
                    P.wait("scalar", "vv", st.get(("pt_msk", kt % 2), 0))
                    P.drain("scalar")
                    gv = grp[:, :].rearrange("p (h q) -> p h q", h=2)
                    pv = ptv.rearrange("p (h q) -> p h q", h=2)
                    es_ = P.op("scalar", "activation", sem="ss",
                               out=pv[:, :, m0 * 128:512],
                               in_=gv[:, :, m0 * 128:512], func=AF.Exp,
                               bias=smf[:, 22 + pr:23 + pr], scale=0.125)
                    st[("exp_ss", gi)] = es_
                    if kt >= 4 * qg:
                        P.wait("vector", "ss", es_)
                        P.drain("vector")
                        P.op("vector", "tensor_tensor",
                             out=ptv[:, m0 * 128:512],
                             in0=ptv[:, m0 * 128:512],
                             in1=g["masksS"][:, m0 * 512 + m0 * 128:
                                             (m0 + 1) * 512],
                             op=OP.mult)
                        mk = P.op("vector", "tensor_tensor", sem="vv",
                                  out=ptv[:, 512 + m0 * 128:1024],
                                  in0=ptv[:, 512 + m0 * 128:1024],
                                  in1=g["masksS"][:, m0 * 512 + m0 * 128:
                                                  (m0 + 1) * 512],
                                  op=OP.mult)
                        st[("pt_msk", kt % 2)] = mk
                        st[("pt_ready", kt)] = ("vv", mk)
                    else:
                        st[("pt_ready", kt)] = ("ss", es_)
            av_mm = P.snap("mm")
            P.wait("vector", "mm", av_mm)
            P.wait("vector", "ss", st.get("lr_ss", 0))
            P.drain("vector")
            c1 = P.op("vector", "tensor_copy", sem="vv", out=scr[0:65, 0:512],
                      in_=obA[0:65, :])
            P.drain("vector")
            c2 = P.op("vector", "tensor_copy", sem="vv",
                      out=scr[32:33, 1536:2048], in_=obB[64:65, :])
            P.drain("vector")
            ofree = P.op("vector", "tensor_copy", sem="vv",
                         out=scr[64:128, 512:1024], in_=obB[0:64, :])
            st["o_free"] = ofree
            P.wait("scalar", "vv", c1)
            P.drain("scalar")
            P.op("scalar", "activation", out=smf[0:1, 512:1024],
                 in_=scr[64:65, 0:512], func=AF.Ln)
            P.wait("scalar", "vv", c2)
            P.op("scalar", "activation", out=smf[32:33, 512:1024],
                 in_=scr[32:33, 1536:2048], func=AF.Ln)
            P.drain("scalar")
            P.op("scalar", "activation", out=smf[0:1, 512:1024],
                 in_=smf[0:1, 512:1024], func=AF.Exp, scale=-1.0)
            lr_ss = P.op("scalar", "activation", sem="ss",
                         out=smf[32:33, 512:1024], in_=smf[32:33, 512:1024],
                         func=AF.Exp, scale=-1.0)
            st["lr_ss"] = lr_ss
            st["pending_norm"] = (pr, qg, lr_ss)
    if st.get("pending_norm"):
        norm_part2(P, g, st.pop("pending_norm"))
    o_done = P.snap("vv")
    st["v_used"] = o_done

    P.wait("gpsimd", "vv", o_done)
    wo = av(g, AV, 2048)
    src = bass.AP(g["dr"]["wo_sh"], p * 2 * 128 * D,
                  [[D, 128], [128 * D, 2], [1, D]])
    P.dma("gpsimd", wo.rearrange("p (a b) -> p a b", a=2), src, "gwa")
    wo_gld = P.snap("gwa")
    ytileA = av(g, APT, 512)
    ytileB = av(g, APT + 512, 512)
    yprev = av(g, APT + 1024, 512)
    yprev2 = av(g, APT + 1536, 512)
    P.wait("sync", "mm", P.snap("mm"))
    P.wait("sync", "vv", P.snap("vv"))
    psum_fence(P)
    yb3 = [g["p4"], g["p5"]]
    for ntg in range(NQG):
        for dcx in range(DC):
            i2 = (ntg * DC + dcx) % 2
            yb2 = yb3[i2]
            ytile = ytileA if i2 == 0 else ytileB
            P.wait("tensor", "gwa", wo_gld)
            P.wait("tensor", "vv", o_done)
            P.wait("tensor", "vv", st.get(("yt_used", i2), 0))
            mv = None
            for oc in range(2):
                mv = P.op("tensor", "matmul", yb2[:, :],
                          wo[:, oc * 1024 + dcx * 128: oc * 1024 + (dcx + 1) * 128],
                          av(g, AO + oc * T + ntg * TT, TT),
                          start=(oc == 0), stop=(oc == 1),
                          sem="mm" if oc == 1 else None)
            bsem = "bna" if i2 == 0 else "bnb"
            if p == 0:
                P.wait("vector", "mm", mv)
                P.wait("vector", bsem, st.get(("yt_sst", i2), 0))
                P.drain("vector")
                ytv = P.op("vector", "tensor_copy", sem="vv", out=ytile,
                           in_=yb2[:, :])
                st[("yt_used", i2)] = ytv
                P.wait("sync", "vv", ytv)
                dst = bass.AP(g["yacc"], dcx * 128 * T + ntg * TT,
                              [[T, 128], [1, TT]])
                P.dma("sync", dst, ytile, bsem)
                st[("yt_sst", i2)] = P.snap(bsem)
            else:
                ypsem = "ypa" if i2 == 0 else "ypb"
                P.wait("sync", "vv", st.get(("yp_used", i2), 0))
                src2 = bass.AP(g["yacc"], dcx * 128 * T + ntg * TT,
                               [[T, 128], [1, TT]])
                P.dma("sync", yprev if i2 == 0 else yprev2, src2, ypsem)
                ypl = P.snap(ypsem)
                P.wait("vector", "mm", mv)
                P.wait("vector", ypsem, ypl)
                P.wait("vector", bsem, st.get(("yt_sst", i2), 0))
                P.drain("vector")
                ytv = P.op("vector", "tensor_tensor", sem="vv", out=ytile,
                           in0=yb2[:, :], in1=yprev if i2 == 0 else yprev2,
                           op=OP.add)
                st[("yt_used", i2)] = ytv
                st[("yp_used", i2)] = ytv
                P.wait("sync", "vv", ytv)
                half = ntg // 4
                dst = bass.AP(g["y_bounce"],
                              (half * D + dcx * 128) * NT + (ntg % 4) * TT,
                              [[NT, 128], [1, TT]])
                P.dma("sync", dst, ytile, bsem)
                st[("yt_sst", i2)] = P.snap(bsem)
    st["wsl_free_mm"] = P.snap("mm")


def emit_output(P, g):
    if "dbg_gates" in g and g["stage"] == "ffn1":
        P.waitc("sync", "vv")
        P.dma("sync", g["dbg_gates"][:, :], g["gates_fm"][0:8, :], "outw")
        P.dma("sync", g["dbg_xt"][:, :], av(g, AXT, 2048), "outw")
        P.dma("sync", g["dbg_h"][:, :], av(g, AH, 2048), "outw")
    P.waitc("sync", "vv")
    for c in range(DC):
        dst = bass.AP(g["out"], c * 128 * NT, [[NT, 128], [1, NT]])
        P.dma("sync", dst, g["x_res"][:, c * NT:(c + 1) * NT], "outw")
    P.waitc("sync", "outw")



def rope_perm():
    perm = np.zeros(D, dtype=np.int64)
    for h in range(H):
        for j in range(DH // 2):
            perm[h * DH + 2 * j] = h * DH + j
            perm[h * DH + 2 * j + 1] = h * DH + DH // 2 + j
    return perm


def host_constants():
    c = {}
    c["ident"] = np.eye(128, dtype=np.float32)
    c["ones_bf"] = np.ones((128, 128), dtype=BF)
    c["onesrow_f"] = np.ones((1, 128), dtype=np.float32)
    ho = np.zeros((128, 8), dtype=BF)
    for qc in range(2):
        for hh in range(2):
            ho[hh * 64:(hh + 1) * 64, qc * 4 + 2 * qc + hh] = 1.0
    ho = np.zeros((128, 8), dtype=BF)
    for qc in range(2):
        for hh in range(2):
            m = 2 * qc + hh
            ho[hh * 64:(hh + 1) * 64, qc * 4 + m] = 1.0
    c["headones"] = ho
    half = DH // 2
    pos = np.arange(T, dtype=np.float64)
    inv = 1.0 / (10000.0 ** (np.arange(half, dtype=np.float64) / half))
    ang = pos[None, :] * inv[:, None]
    cosr = np.cos(ang)
    sinr = np.sin(ang)
    cos2 = np.zeros((128, T), dtype=np.float32)
    sin2 = np.zeros((128, T), dtype=np.float32)
    for r in range(128):
        j = (r % 64) // 2
        cos2[r] = cosr[j]
        sin2[r] = sinr[j] if (r % 2 == 1) else -sinr[j]
    c["cos2"] = cos2.astype(BF)
    c["sin2"] = sin2.astype(BF)
    masks = np.zeros((128, 4 * TT), dtype=BF)
    for m in range(4):
        kpos = m * 128 + np.arange(128)[:, None]
        qpos = np.arange(TT)[None, :]
        masks[:, m * TT:(m + 1) * TT] = (kpos <= qpos).astype(BF)
    c["masks"] = masks
    cc = np.zeros((128, 4), dtype=np.float32)
    cc[:, 0] = EPS
    cc[:, 1] = 1.0
    c["ccol"] = cc
    sel = np.zeros((8, E * 128), dtype=BF)
    for e in range(E):
        sel[e, e * 128:(e + 1) * 128] = 1.0
    c["sel8"] = sel
    s2 = np.zeros((33, 128), dtype=np.float32)
    s2[0, 0:64] = 1.0
    s2[32, 64:128] = 1.0
    c["sel2"] = s2
    return c


_NC_CACHE = {}


def kernel(L0_attn_x, L0_attn_W_q, L0_attn_W_dkv, L0_attn_W_uk, L0_attn_W_uv,
           L0_attn_W_o, L0_attn_ln_gamma, L0_attn_ln_beta, L0_ffn_gate_w,
           L0_ffn_gate_bias, L0_ffn_w1, L0_ffn_w3, L0_ffn_w2, L0_ffn_sw1,
           L0_ffn_sw3, L0_ffn_sw2, stage="full"):
    perm = rope_perm()
    wq_p = np.ascontiguousarray(np.asarray(L0_attn_W_q)[:, perm])
    wuk_p = np.ascontiguousarray(np.asarray(L0_attn_W_uk)[:, perm])
    con = host_constants()

    base = dict(con)
    base["w1t"] = np.ascontiguousarray(np.asarray(L0_ffn_w1).transpose(0, 2, 1))
    base["w3t"] = np.ascontiguousarray(np.asarray(L0_ffn_w3).transpose(0, 2, 1))
    base["w2t"] = np.ascontiguousarray(np.asarray(L0_ffn_w2).transpose(0, 2, 1))
    base["sw1t"] = np.ascontiguousarray(np.asarray(L0_ffn_sw1).T)
    base["sw3t"] = np.ascontiguousarray(np.asarray(L0_ffn_sw3).T)
    base["sw2t"] = np.ascontiguousarray(np.asarray(L0_ffn_sw2).T)
    base["gwt"] = np.ascontiguousarray(np.asarray(L0_ffn_gate_w).T)
    base["gbias"] = np.asarray(L0_ffn_gate_bias).reshape(1, E).astype(np.float32)
    base["wdkv"] = np.ascontiguousarray(np.asarray(L0_attn_W_dkv))
    lg = np.asarray(L0_attn_ln_gamma).reshape(LC, 128).T
    lb = np.asarray(L0_attn_ln_beta).reshape(LC, 128).T
    base["ln_g"] = np.ascontiguousarray(lg.astype(np.float32))
    base["ln_b"] = np.ascontiguousarray(lb.astype(np.float32))

    x = np.asarray(L0_attn_x)
    x_fm = x.transpose(0, 2, 1)

    in_maps = []
    for core in range(8):
        b, hh = core // 2, core % 2
        m = dict(base)
        m["x_sh"] = np.ascontiguousarray(x_fm[b, :, hh * NT:(hh + 1) * NT])
        hc = slice(hh * HL * DH, (hh + 1) * HL * DH)
        m["wq_sh"] = np.ascontiguousarray(wq_p[:, hc])
        m["wuk_sh"] = np.ascontiguousarray(wuk_p[:, hc])
        m["wuv_sh"] = np.ascontiguousarray(np.asarray(L0_attn_W_uv)[:, hc])
        m["wo_sh"] = np.ascontiguousarray(np.asarray(L0_attn_W_o)[hc, :])
        in_maps.append(m)

    key = stage
    if key not in _NC_CACHE:
        _NC_CACHE[key] = build_program(stage)
    nc = _NC_CACHE[key]
    res = run_bass_kernel_spmd(nc, in_maps, core_ids=list(range(8)))
    outs = []
    for b in range(BATCH):
        lo = res.results[2 * b]["out"]
        hi = res.results[2 * b + 1]["out"]
        outs.append(np.concatenate([lo, hi], axis=1).T)
    return np.stack(outs).astype(np.float32)


# version 16 (final)
# speedup vs baseline: 1.0766x; 1.0169x over previous
import numpy as np
import ml_dtypes
from contextlib import ExitStack

import concourse.bass as bass
import concourse.mybir as mybir
from concourse.alu_op_type import AluOpType as OP
from concourse.bass_utils import run_bass_kernel_spmd

DEBUG_RACES = False

F32 = mybir.dt.float32
BF16 = mybir.dt.bfloat16
AF = mybir.ActivationFunctionType
AXX = mybir.AxisListType.X
BF = ml_dtypes.bfloat16

D = 1024
T = 4096
BATCH = 4
H = 16
DH = 64
LAT = 256
E = 8
ISH = 2048
NT = 2048
TT = 512
NTT = NT // TT
DC = D // 128
LC = LAT // 128
HL = H // 2
EPS = 1e-5
NQG = T // TT
NKT = T // 128
NEX = E + 2

RG = [[0, 1], [2, 3], [4, 5], [6, 7]]

AXT = 0
AH = 16384
ALN = 16384
AW1 = 32768
AW3 = 40960
AW2 = 49152
ARENA = 53248
ALAT = 0
AQ = 8192
AK = 16384
AV = 24576
APT = 32896
AO = 34944
AWS = 43136
AXTI = 47232
ASP = 51328


class Pass:

    def __init__(self, which, eng, semh):
        self.which = which
        self.eng = eng
        self.semh = semh
        self.c = {k: 0 for k in semh}

    def op(self, engname, fname, *args, sem=None, inc=1, **kw):
        if sem is not None:
            self.c[sem] += inc
        if self.which == engname:
            inst = getattr(self.eng, fname)(*args, **kw)
            if sem is not None:
                inst.then_inc(self.semh[sem], inc)
        return self.c[sem] if sem is not None else None

    def dma(self, engname, out, in_, sem):
        return self.op(engname, "dma_start", sem=sem, inc=16, out=out, in_=in_)

    def wait(self, engname, sem, val):
        if val is None or val <= 0:
            return
        if self.which == engname:
            self.eng.wait_ge(self.semh[sem], val)

    def waitc(self, engname, sem):
        self.wait(engname, sem, self.c[sem])

    def drain(self, engname):
        if self.which == engname:
            self.eng.drain()

    def snap(self, sem):
        return self.c[sem]


def build_program(stage="full"):
    nc = bass.Bass("TRN2", target_bir_lowering=False,
                   detect_race_conditions=DEBUG_RACES)
    dr = {}

    def din(name, shape, dt=F32):
        dr[name] = nc.dram_tensor(name, list(shape), dt, kind="ExternalInput")

    din("x_sh", [D, NT])
    din("w1t", [E, D, D])
    din("w3t", [E, D, D])
    din("w2t", [E, D, D])
    din("sw1t", [D, ISH])
    din("sw3t", [D, ISH])
    din("sw2t", [ISH, D])
    din("gwt", [D, E])
    din("gbias", [1, E])
    din("wq_sh", [D, HL * DH])
    din("wdkv", [D, LAT])
    din("wuk_sh", [LAT, HL * DH])
    din("wuv_sh", [LAT, HL * DH])
    din("wo_sh", [HL * DH, D])
    din("ln_g", [128, LC])
    din("ln_b", [128, LC])
    din("ident", [128, 128])
    din("ones_bf", [128, 128], BF16)
    din("onesrow_f", [1, 128])
    din("headones", [128, 8], BF16)
    din("cos2", [128, T], BF16)
    din("sin2", [128, T], BF16)
    din("masks", [128, 4 * TT], BF16)
    din("ccol", [128, 4])
    din("sel8", [8, E * 128], BF16)
    din("sel2", [33, 128])

    out_t = nc.dram_tensor("out", [D, NT], F32, kind="ExternalOutput")
    if stage in ("ffn1",):
        g_dbg = nc.dram_tensor("dbg_gates", [8, NT], BF16, kind="ExternalOutput")
        x_dbg = nc.dram_tensor("dbg_xt", [128, 2048], BF16, kind="ExternalOutput")
        h_dbg = nc.dram_tensor("dbg_h", [128, 2048], BF16, kind="ExternalOutput")

    g0 = dict(dr=dr, out=out_t, stage=stage)
    if stage in ("ffn1",):
        g0["dbg_gates"] = g_dbg
        g0["dbg_xt"] = x_dbg
        g0["dbg_h"] = h_dbg
    g0["w1s"] = nc.dram_tensor("w1s", [NEX, 128, 8192], BF16)
    g0["w3s"] = nc.dram_tensor("w3s", [NEX, 128, 8192], BF16)
    g0["w2s"] = nc.dram_tensor("w2s", [NEX, 128, 8192], BF16)
    g0["xt_bounce"] = nc.dram_tensor("xt_bounce", [D, NT], BF16)
    g0["xt_gath"] = nc.dram_tensor("xt_gath", [2 * D, NT], BF16)
    g0["yacc"] = nc.dram_tensor("yacc", [D, T], BF16)
    g0["y_bounce"] = nc.dram_tensor("y_bounce", [2 * D, NT], BF16)
    g0["y_red"] = nc.dram_tensor("y_red", [D, NT], BF16)

    es = ExitStack()
    with es:
        def sb(name, shape, dt):
            g0[name] = es.enter_context(nc.sbuf_tensor(name, list(shape), dt))

        sb("x_res", [128, DC * NT], F32)
        sb("A", [128, ARENA], BF16)
        sb("scr", [128, 2048], F32)
        sb("smf", [128, 1024], F32)
        sb("identS", [128, 128], F32)
        sb("ones_bfS", [128, 128], BF16)
        sb("onesrow_fS", [1, 128], F32)
        sb("headonesS", [128, 8], BF16)
        sb("cos2S", [128, T], BF16)
        sb("sin2S", [128, T], BF16)
        sb("masksS", [128, 4 * TT], BF16)
        sb("ccolS", [128, 4], F32)
        sb("gwt_sb", [128, DC * E], F32)
        sb("gb_sb", [1, E], F32)
        sb("lng_sb", [128, LC], F32)
        sb("lnb_sb", [128, LC], F32)
        sb("gates_fm", [8, NT], BF16)
        sb("sel8S", [8, E * 128], BF16)
        sb("small2", [128, 48], F32)

        for nm, shape in [("pbig0", [128, 1024]), ("pbig1", [128, 1024]),
                          ("pbig2", [128, 1024]),
                          ("p6", [128, 512]), ("p7", [128, 512])]:
            g0[nm] = es.enter_context(nc.psum_tensor(nm, shape, F32))
        g0["p4"] = g0["pbig2"][:, 0:512]
        g0["p5"] = g0["pbig2"][:, 512:1024]

        sems = ["cst", "xld", "sw1", "sw3", "sw2", "tw1", "tw3", "tw2",
                "xta", "xtb", "ypa", "ypb", "yra", "yrb", "bnc", "bna",
                "bnb", "gw1", "gw3", "gw2", "gwa", "scd", "outw", "sl2",
                "xl2", "mm", "vv", "ss", "cc", "gg"]
        semh = {n: es.enter_context(nc.semaphore(n)) for n in sems}
        blk = es.enter_context(nc.Block())

        def run(which):
            def f(e):
                program(Pass(which, e, semh), g0)
            return f

        blk.sync(run("sync"))
        blk.tensor(run("tensor"))
        blk.vector(run("vector"))
        blk.scalar(run("scalar"))
        blk.gpsimd(run("gpsimd"))

    return nc



def xr(g, c, t0, t1):
    return g["x_res"][:, c * NT + t0: c * NT + t1]


def av(g, off, n):
    return g["A"][:, off: off + n]


def avr(g, off, n, p0, p1):
    return g["A"][p0:p1, off: off + n]


def program(P, g0):
    g = dict(g0)
    g["_st"] = {}
    stage = g["stage"]

    emit_const_loads(P, g)
    emit_xload(P, g)

    emit_ffn_sublayer(P, g, app=0)
    if stage == "ffn1":
        emit_output(P, g)
        return
    emit_attention(P, g)
    if stage == "attn":
        emit_output(P, g)
        return
    nex2 = NEX
    if stage.startswith("full") and len(stage) > 4:
        nex2 = int(stage[4:])
    emit_ffn_sublayer(P, g, app=1, nex=nex2)
    emit_output(P, g)


def emit_const_loads(P, g):
    dr = g["dr"]
    for s, d in [("identS", "ident"), ("ones_bfS", "ones_bf"),
                 ("onesrow_fS", "onesrow_f"), ("headonesS", "headones"),
                 ("cos2S", "cos2"), ("sin2S", "sin2"), ("masksS", "masks"),
                 ("ccolS", "ccol"), ("gb_sb", "gbias"), ("lng_sb", "ln_g"),
                 ("lnb_sb", "ln_b"), ("sel8S", "sel8")]:
        P.dma("sync", g[s][:, :], dr[d][:, :], "cst")
    src = bass.AP(dr["gwt"], 0, [[E, 128], [128 * E, DC], [1, E]])
    P.dma("sync", g["gwt_sb"][:, :].rearrange("p (c e) -> p c e", c=DC), src, "cst")
    g["const_sld"] = P.snap("cst")


def emit_xload(P, g):
    src = bass.AP(g["dr"]["x_sh"], 0, [[NT, 128], [128 * NT, 4], [1, NT]])
    P.dma("sync", g["x_res"][:, 0:4 * NT].rearrange("p (c t) -> p c t", c=4),
          src, "xld")
    g["xload_sld"] = P.snap("xld")
    src = bass.AP(g["dr"]["x_sh"], 4 * 128 * NT, [[NT, 128], [128 * NT, 4], [1, NT]])
    P.dma("sync", g["x_res"][:, 4 * NT:].rearrange("p (c t) -> p c t", c=4),
          src, "xl2")
    g["xload2"] = P.snap("xl2")



def emit_ln(P, g, with_gate, bounce=None):
    A, scr, smf = g["A"], g["scr"], g["smf"]
    S_ps, Q_ps = g["p4"], g["p5"]
    st = g["_st"]
    P.wait("vector", "mm", P.snap("mm"))
    P.wait("scalar", "mm", P.snap("mm"))
    P.wait("vector", "gg", P.snap("gg"))
    P.wait("scalar", "gg", P.snap("gg"))
    P.wait("vector", "xld", P.snap("xld"))
    P.wait("vector", "xl2", P.snap("xl2"))
    P.drain("vector")
    P.drain("scalar")
    ln_mm = {}
    for nt in range(NTT):
        for c in range(DC):
            xb = av(g, ALN + (c % 2) * 512, 512)
            xq = av(g, ALN + 1024 + (c % 2) * 512, 512)
            if c >= 2:
                P.wait("vector", "mm", ln_mm[c - 2])
            if c < 4:
                P.wait("vector", "xld", g["xload_sld"])
            else:
                P.wait("vector", "xl2", g["xload2"])
            P.drain("vector")
            P.op("vector", "tensor_copy", sem="vv", out=xb,
                 in_=xr(g, c, nt * TT, (nt + 1) * TT))
            P.waitc("scalar", "vv")
            P.drain("scalar")
            P.op("scalar", "activation", sem="ss", out=xq, in_=xb, func=AF.Square)
            P.waitc("tensor", "vv")
            P.waitc("tensor", "ss")
            P.wait("tensor", "cst", g["const_sld"])
            P.op("tensor", "matmul", S_ps[:, :], g["ones_bfS"][:, :], xb,
                 start=(c == 0), stop=(c == DC - 1))
            ln_mm[c] = P.op("tensor", "matmul", Q_ps[:, :], g["ones_bfS"][:, :],
                            xq, start=(c == 0), stop=(c == DC - 1), sem="mm")
        stats_mm = P.snap("mm")
        mu2, var, xm, rstd = (scr[:, 0:512], scr[:, 512:1024],
                              scr[:, 1024:1536], scr[:, 1536:2048])
        P.wait("scalar", "mm", stats_mm)
        P.drain("scalar")
        P.op("scalar", "activation", sem="ss", out=mu2, in_=S_ps[:, :],
             func=AF.Square, scale=1.0 / D)
        P.waitc("vector", "ss")
        P.wait("vector", "mm", stats_mm)
        P.drain("vector")
        P.op("vector", "scalar_tensor_tensor", sem="vv", out=var, in0=Q_ps[:, :],
             scalar=1.0 / D, in1=mu2, op0=OP.mult, op1=OP.subtract)
        P.waitc("scalar", "vv")
        P.drain("scalar")
        P.op("scalar", "activation", sem="ss", out=rstd, in_=var, func=AF.Ln,
             bias=g["ccolS"][:, 0:1], scale=1.0)
        P.drain("scalar")
        P.op("scalar", "activation", sem="ss", out=rstd, in_=rstd, func=AF.Exp,
             scale=-0.5)
        recip_vv = P.snap("vv")
        rstd_ss = P.snap("ss")
        P.wait("vector", "ss", rstd_ss)
        for c in range(DC):
            P.drain("vector")
            P.op("vector", "scalar_tensor_tensor", out=xm, in0=S_ps[:, :],
                 scalar=-1.0 / D, in1=xr(g, c, nt * TT, (nt + 1) * TT),
                 op0=OP.mult, op1=OP.add)
            P.drain("vector")
            xtv = av(g, AXT + c * NT + nt * TT, TT)
            xt_vv = P.op("vector", "tensor_tensor", sem="vv", out=xtv, in0=xm,
                         in1=rstd, op=OP.mult)
            if with_gate:
                x32 = smf[:, (c % 2) * 512:(c % 2) * 512 + 512]
                if c >= 2:
                    P.wait("vector", "mm", st.get(("gmm", c - 2), 0))
                P.drain("vector")
                P.op("vector", "tensor_tensor", sem="vv", out=x32, in0=xm,
                     in1=rstd, op=OP.mult)
                P.waitc("tensor", "vv")
                gv = P.op("tensor", "matmul", g["p6"][0:E, :],
                          g["gwt_sb"][:, c * E:(c + 1) * E], x32,
                          start=(c == 0), stop=(c == DC - 1), sem="mm")
                st[("gmm", c)] = gv
            if bounce is not None:
                eng = "sync" if c % 2 == 0 else "scalar"
                q = "bnc" if c % 2 == 0 else "scd"
                P.wait(eng, "vv", xt_vv)
                dst = bass.AP(bounce, c * 128 * NT + nt * TT, [[NT, 128], [1, TT]])
                P.dma(eng, dst, xtv, q)
        if with_gate:
            emit_gate_math(P, g, nt)
    g["xt_vv"] = P.snap("vv")


def emit_gate_math(P, g, nt):
    scr, smf, small2 = g["scr"], g["smf"], g["small2"]
    G_ps, T_ps = g["p6"], g["p7"]
    st = g["_st"]
    gate_mm = P.snap("mm")
    gsc = scr[0:E, 0:512]
    P.wait("vector", "mm", gate_mm)
    P.drain("vector")
    P.op("vector", "tensor_copy", sem="vv", out=gsc, in_=G_ps[0:E, :])
    gevac = P.snap("vv")

    if not st.get("bias8"):
        st["bias8"] = True
        P.wait("tensor", "cst", g["const_sld"])
        b8v = P.op("tensor", "matmul", T_ps[:, 0:E], g["onesrow_fS"][:, :],
                   g["gb_sb"][:, :], start=True, stop=True, sem="mm")
        P.wait("vector", "mm", b8v)
        P.drain("vector")
        P.op("vector", "tensor_copy", sem="vv", out=small2[:, 32:40],
             in_=T_ps[:, 0:E])
        st["bias8_vv"] = P.snap("vv")
    bias8 = small2[:, 32:40]

    for sub in range(4):
        gt = T_ps[:, 8 + sub * 8: 16 + sub * 8]
        P.wait("tensor", "vv", gevac)
        trv = P.op("tensor", "transpose", gt, gsc[:, sub * 128:(sub + 1) * 128],
                   g["identS"][0:E, 0:E], sem="mm")
        nmax, sum8, rec = (smf[:, 0:1], smf[:, 1:2], smf[:, 2:3])
        e8, b8, m8, msk, gtm = (small2[:, 0:8], small2[:, 8:16],
                                small2[:, 16:24], small2[:, 24:32],
                                small2[:, 40:48])
        P.wait("vector", "mm", trv)
        P.drain("vector")
        P.op("vector", "reduce_max", sem="vv", out=nmax, in_=gt, axis=AXX,
             negate=True)
        P.waitc("scalar", "vv")
        P.drain("scalar")
        P.op("scalar", "activation", sem="ss", out=e8, in_=gt, func=AF.Exp,
             bias=nmax, accum_out=sum8)
        P.waitc("vector", "ss")
        P.drain("vector")
        P.op("vector", "reciprocal", out=rec, in_=sum8)
        P.drain("vector")
        P.op("vector", "tensor_scalar", out=e8, in0=e8, scalar1=rec,
             scalar2=None, op0=OP.mult)
        P.drain("vector")
        P.wait("vector", "vv", st["bias8_vv"])
        P.op("vector", "tensor_tensor", out=b8, in0=e8, in1=bias8, op=OP.add)
        P.drain("vector")
        P.op("vector", "max", out=m8, in_=b8)
        P.drain("vector")
        P.op("vector", "tensor_scalar", out=msk, in0=b8, scalar1=m8[:, 1:2],
             scalar2=None, op0=OP.is_ge)
        P.drain("vector")
        P.op("vector", "tensor_tensor", sem="vv", out=gtm, in0=e8, in1=msk,
             op=OP.mult)
        P.waitc("tensor", "vv")
        tb = P.op("tensor", "transpose", G_ps[0:8, sub * 128:(sub + 1) * 128],
                  gtm, g["identS"][:, :], sem="mm")
        P.wait("vector", "mm", tb)
        P.drain("vector")
        P.op("vector", "tensor_copy", sem="vv",
             out=g["gates_fm"][0:8, nt * TT + sub * 128: nt * TT + (sub + 1) * 128],
             in_=G_ps[0:8, sub * 128:(sub + 1) * 128])
    g["gates_vv"] = P.snap("vv")



def wsrc_ap(g, kind, e, idx):
    dr = g["dr"]
    if kind in ("w1", "w3"):
        if e < E:
            return bass.AP(dr[kind + "t"], e * D * D + idx * 512,
                           [[D, 128], [128 * D, DC], [1, 512]])
        return bass.AP(dr["s" + kind + "t"], (e - E) * 1024 + idx * 512,
                       [[ISH, 128], [128 * ISH, DC], [1, 512]])
    if e < E:
        return bass.AP(dr["w2t"], e * D * D + idx * 256,
                       [[D, 128], [128 * D, DC], [1, 256]])
    return bass.AP(dr["sw2t"], (e - E) * 8 * 128 * D + idx * 256,
                   [[D, 128], [128 * D, DC], [1, 256]])


def stash_ap(g, kind, e, idx):
    n = 4096 if kind in ("w1", "w3") else 2048
    return g[kind + "s"][e, :, idx * n:(idx + 1) * n]


def wslot(g, kind, slot):
    if kind == "w1":
        return av(g, AW1 + slot * 4096, 4096)
    if kind == "w3":
        return av(g, AW3 + slot * 4096, 4096)
    return av(g, AW2 + slot * 2048, 2048)


def emit_wload(P, g, app, kind, e, idx, slot, wait_mm=None, wait_vv=None):
    sv = wslot(g, kind, slot)
    gsem = {"w1": "gw1", "w3": "gw3", "w2": "gw2"}[kind]
    ssem = {"w1": "sw1", "w3": "sw3", "w2": "sw2"}[kind]
    tsem = {"w1": "tw1", "w3": "tw3", "w2": "tw2"}[kind]
    st = g["_st"]
    if app == 0:
        P.wait("gpsimd", "mm", wait_mm)
        P.wait("gpsimd", "vv", wait_vv)
        P.wait("gpsimd", tsem, st.get(("stw", kind, slot), 0))
        P.waitc("gpsimd", gsem)
        dsh = sv.rearrange("p (a b) -> p a b", a=DC)
        P.dma("gpsimd", dsh, wsrc_ap(g, kind, e, idx), gsem)
        v = P.snap(gsem)
        P.wait("sync", gsem, v)
        P.waitc("sync", tsem)
        P.dma("sync", stash_ap(g, kind, e, idx), sv, tsem)
        st[("stw", kind, slot)] = P.snap(tsem)
        return (gsem, v)
    P.wait("sync", "mm", wait_mm)
    P.wait("sync", "vv", wait_vv)
    P.waitc("sync", ssem)
    P.dma("sync", sv, stash_ap(g, kind, e, idx), ssem)
    return (ssem, P.snap(ssem))


def psum_fence(P):
    P.wait("tensor", "ss", P.snap("ss"))
    P.wait("tensor", "vv", P.snap("vv"))


def emit_ffn_sublayer(P, g, app, nex=NEX):
    emit_ln(P, g, with_gate=True)
    st = g["_st"]
    scr = g["scr"]
    hpairs = [(g["pbig0"][:, 0:512], g["pbig0"][:, 512:1024]),
              (g["pbig1"][:, 0:512], g["pbig1"][:, 512:1024]),
              (g["p4"][:, :], g["p5"][:, :])]
    Gb = g["p6"]
    yb = [g["p4"], g["p5"], g["p7"]]
    ones1_bf = g["ones_bfS"][0:1, :]
    cvring = []
    yring = []
    silu_ring = [scr[:, 0:256].bitcast(BF16), scr[:, 256:512].bitcast(BF16),
                 scr[:, 512:768].bitcast(BF16)]
    tmp_ring = [scr[:, 768:1024].bitcast(BF16), scr[:, 1024:1280].bitcast(BF16)]
    uc = [0]

    psum_fence(P)
    first_mm = P.snap("mm")
    first_vv = P.snap("vv")
    ld = {}
    if app == 1 and "preld" in g:
        ld.update(g.pop("preld"))
    else:
        for kind, idx, slot in [("w1", 0, 0), ("w3", 0, 0), ("w2", 0, 0),
                                ("w1", 1, 1), ("w3", 1, 1), ("w2", 1, 1)]:
            ld[(0, kind, idx)] = emit_wload(P, g, app, kind, 0, idx, slot,
                                            wait_mm=first_mm, wait_vv=first_vv)

    for ei in range(nex):
        is_sh = ei >= E
        psum_fence(P)
        for nt in range(NTT):
            if not is_sh:
                P.wait("tensor", "vv", g["gates_vv"])
                P.wait("tensor", "gg", st.get("gbc_used", 0))
                P.wait("tensor", "ss", st.get("gbc_evac", 0))
                gbv = P.op("tensor", "matmul", Gb[:, :],
                           g["sel8S"][:, ei * 128:(ei + 1) * 128],
                           g["gates_fm"][0:8, nt * TT:(nt + 1) * TT],
                           start=True, stop=True, sem="mm")
                st["gbc_mm"] = gbv
                P.wait("scalar", "mm", gbv)
                P.wait("scalar", "gg", st.get("gbc_used", 0))
                P.drain("scalar")
                st["gbc_evac"] = P.op("scalar", "copy", sem="ss",
                                      out=g["smf"][:, 0:512], in_=Gb[:, :])
            for ic in range(DC):
                u = uc[0]; uc[0] += 1
                hb, h3 = hpairs[u % 3]
                hf, icl = ic // 4, ic % 4
                w1v, w3v = wslot(g, "w1", hf), wslot(g, "w3", hf)
                s1, v1 = ld[(ei, "w1", hf)]
                s3, v3 = ld[(ei, "w3", hf)]
                P.wait("tensor", s1, v1)
                P.wait("tensor", s3, v3)
                if len(cvring) >= 3:
                    P.wait("tensor", "vv", cvring[-3][0])
                    P.wait("tensor", "ss", cvring[-3][2])
                for kc in range(DC):
                    P.op("tensor", "matmul", hb,
                         w1v[:, kc * 512 + icl * 128: kc * 512 + (icl + 1) * 128],
                         av(g, AXT + kc * NT + nt * TT, TT),
                         start=(kc == 0), stop=(kc == DC - 1))
                mv = None
                for kc in range(DC):
                    mv = P.op("tensor", "matmul", h3,
                              w3v[:, kc * 512 + icl * 128: kc * 512 + (icl + 1) * 128],
                              av(g, AXT + kc * NT + nt * TT, TT),
                              start=(kc == 0), stop=(kc == DC - 1),
                              sem="mm" if kc == DC - 1 else None)
                silu = silu_ring[u % 3]
                P.wait("scalar", "mm", mv)
                P.drain("scalar")
                sse = P.op("scalar", "activation", sem="ss", out=silu, in_=hb,
                           func=AF.Silu)
                hout = av(g, AH + ic * NT + nt * TT, TT)
                P.waitc("vector", "ss")
                if not is_sh:
                    last2 = (nt == NTT - 1 and ic >= DC - 2)
                    tmp = tmp_ring[u % 2]
                    if len(cvring) >= 2 and cvring[-2][1]:
                        P.wait("vector", "gg", cvring[-2][1])
                    P.drain("vector")
                    tv = P.op("vector", "tensor_tensor", sem="vv", out=tmp,
                              in0=h3, in1=silu, op=OP.mult)
                    if last2:
                        P.wait("vector", "ss", st["gbc_evac"])
                        if len(cvring) >= 2 and cvring[-2][1]:
                            P.wait("vector", "gg", cvring[-2][1])
                        P.drain("vector")
                        cv = P.op("vector", "tensor_tensor", sem="vv",
                                  out=hout, in0=tmp, in1=g["smf"][:, 0:512],
                                  op=OP.mult)
                        cvring.append((cv, None, sse))
                    else:
                        P.wait("gpsimd", "vv", tv)
                        P.wait("gpsimd", "ss", st["gbc_evac"])
                        if len(cvring) >= 2:
                            P.wait("gpsimd", "gg", cvring[-2][1])
                        cv = P.op("gpsimd", "tensor_tensor", sem="gg", out=hout,
                                  in0=tmp, in1=g["smf"][:, 0:512], op=OP.mult)
                        cvring.append((tv, cv, sse))
                else:
                    P.drain("vector")
                    cv = P.op("vector", "tensor_tensor", sem="vv", out=hout,
                              in0=h3, in1=silu, op=OP.mult)
                    cvring.append((cv, None, sse))
                if nt == NTT - 1 and ic == 3 and ei + 1 < nex:
                    for kind in ("w1", "w3"):
                        ld[(ei + 1, kind, 0)] = emit_wload(
                            P, g, app, kind, ei + 1, 0, 0, wait_mm=mv)
            st["gbc_used"] = P.snap("gg")
        a_done_mm = P.snap("mm")
        h_done_vv = P.snap("vv")
        h_done_gg = P.snap("gg")
        if ei + 1 < nex:
            for kind in ("w1", "w3"):
                ld[(ei + 1, kind, 1)] = emit_wload(
                    P, g, app, kind, ei + 1, 1, 1, wait_mm=a_done_mm)
        psum_fence(P)
        P.wait("tensor", "gg", P.snap("gg"))
        for dcx in range(DC):
            qv = dcx // 2
            s2, v2 = ld[(ei, "w2", qv)]
            for nt in range(NTT):
                ybank = yb[(dcx * NTT + nt) % 3]
                P.wait("tensor", s2, v2)
                P.wait("tensor", "vv", h_done_vv)
                P.wait("tensor", "gg", h_done_gg)
                if len(yring) >= 3:
                    P.wait("tensor", "vv", yring[-3])
                mv = None
                for icx in range(DC):
                    mv = P.op("tensor", "matmul", ybank[:, :],
                              wslot(g, "w2", qv % 2)[:, icx * 256 + (dcx % 2) * 128:
                                                     icx * 256 + (dcx % 2) * 128 + 128],
                              av(g, AH + icx * NT + nt * TT, TT),
                              start=(icx == 0), stop=(icx == DC - 1),
                              sem="mm" if icx == DC - 1 else None)
                P.wait("vector", "mm", mv)
                P.drain("vector")
                yv = P.op("vector", "scalar_tensor_tensor", sem="vv",
                          out=xr(g, dcx, nt * TT, (nt + 1) * TT), in0=ybank[:, :],
                          scalar=0.5, in1=xr(g, dcx, nt * TT, (nt + 1) * TT),
                          op0=OP.mult, op1=OP.add)
                yring.append(yv)
            if dcx % 2 == 1 and dcx < 4:
                nq = dcx // 2 + 2
                ld[(ei, "w2", nq)] = emit_wload(P, g, app, "w2", ei, nq,
                                                nq % 2, wait_mm=P.snap("mm"))
            if dcx == DC - 1 and ei + 1 < nex:
                for q2 in range(2):
                    ld[(ei + 1, "w2", q2)] = emit_wload(
                        P, g, app, "w2", ei + 1, q2, q2, wait_mm=P.snap("mm"))
    g["xres_writer_vv"] = P.snap("vv")



def emit_attention(P, g):
    emit_ln(P, g, with_gate=False, bounce=g["xt_bounce"])
    st = g["_st"]
    scr, smf = g["scr"], g["smf"]

    P.waitc("gpsimd", "bnc")
    P.waitc("gpsimd", "scd")
    P.waitc("sync", "tw1")
    P.waitc("sync", "tw3")
    P.waitc("sync", "tw2")
    P.waitc("sync", "gw1")
    P.waitc("sync", "gw3")
    P.waitc("sync", "gw2")
    P.op("gpsimd", "collective_compute", "AllGather", OP.bypass,
         replica_groups=RG, ins=[g["xt_bounce"][:, :].opt()],
         outs=[g["xt_gath"][:, :].opt()], sem="cc")
    g["ag_cc"] = P.snap("cc")
    if not g["_st"].get("sel2_loaded"):
        g["_st"]["sel2_loaded"] = True
        P.dma("sync", avr(g, ASP + 1536, 256, 0, 33).bitcast(F32),
              g["dr"]["sel2"][:, :], "sl2")
        g["sel2_v"] = P.snap("sl2")

    P.wait("gpsimd", "mm", P.snap("mm"))
    P.waitc("gpsimd", "tw1")
    P.waitc("gpsimd", "tw3")
    P.waitc("gpsimd", "tw2")
    wk = av(g, AWS, 4096)
    src = bass.AP(g["dr"]["wdkv"], 0, [[LAT, 128], [128 * LAT, DC], [1, LAT]])
    P.dma("gpsimd", wk[:, 0:DC * LAT].rearrange("p (a b) -> p a b", a=DC),
          src, "gwa")
    wdkv_gld = P.snap("gwa")

    lat_ps = [g["p4"], g["p5"]]
    psum_fence(P)
    xt_used = [0, 0]
    W2 = 512
    for ntg in range(NQG):
        half = ntg // 4
        xti = av(g, AO + (ntg % 2) * 4096, 4096)
        xsem = "xta" if ntg % 2 == 0 else "xtb"
        P.wait("sync", "cc", g["ag_cc"])
        P.wait("sync", "mm", xt_used[ntg % 2])
        src = bass.AP(g["xt_gath"], half * D * NT + (ntg % 4) * W2,
                      [[NT, 128], [128 * NT, DC], [1, W2]])
        P.dma("sync", xti.rearrange("p (c j) -> p c j", c=DC), src, xsem)
        xt_sld = P.snap(xsem)
        for lc in range(LC):
            P.wait("tensor", xsem, xt_sld)
            P.wait("tensor", "gwa", wdkv_gld)
            P.wait("tensor", "vv", st.get(("lat_evac", ntg - 1, lc), 0))
            for c in range(DC):
                P.op("tensor", "matmul", lat_ps[lc][:, 0:W2],
                     wk[:, c * LAT + lc * 128: c * LAT + (lc + 1) * 128],
                     xti[:, c * W2:(c + 1) * W2],
                     start=(c == 0), stop=(c == DC - 1),
                     sem="mm" if c == DC - 1 else None)
            st[("lat_mm", ntg, lc)] = P.snap("mm")
        xt_used[ntg % 2] = P.snap("mm")
        emit_lat_ln(P, g, ntg, lat_ps, W2)
    st[("xti_used", 0)] = xt_used[0]
    st[("xti_used", 1)] = xt_used[1]
    g["lat_vv"] = P.snap("vv")
    st["wsl_free_mm"] = P.snap("mm")

    for p in range(2):
        emit_attn_pass(P, g, p)

    if g["stage"] == "full":
        pmm, pvv = P.snap("mm"), P.snap("vv")
        preld = {}
        for kind, idx, slot in [("w1", 0, 0), ("w3", 0, 0), ("w2", 0, 0),
                                ("w1", 1, 1), ("w3", 1, 1), ("w2", 1, 1)]:
            preld[(0, kind, idx)] = emit_wload(P, g, 1, kind, 0, idx, slot,
                                               wait_mm=pmm, wait_vv=pvv)
        g["preld"] = preld

    P.waitc("gpsimd", "bna")
    P.waitc("gpsimd", "bnb")
    P.op("gpsimd", "collective_compute", "ReduceScatter", OP.add,
         replica_groups=RG, ins=[g["y_bounce"][:, :].opt()],
         outs=[g["y_red"][:, :].opt()], sem="cc")
    rs_cc = P.snap("cc")
    ytA = av(g, ALAT, 512)
    ytB = av(g, ALAT + 512, 512)
    for dcx in range(DC):
        for nt in range(NTT):
            par = (dcx * NTT + nt) % 2
            yt = ytA if par == 0 else ytB
            ysem = "yra" if par == 0 else "yrb"
            P.wait("sync", "cc", rs_cc)
            P.wait("sync", "vv", st.get(("yred_used", par), 0))
            src = bass.AP(g["y_red"], dcx * 128 * NT + nt * TT,
                          [[NT, 128], [1, TT]])
            P.dma("sync", yt, src, ysem)
            ysld = P.snap(ysem)
            P.wait("vector", ysem, ysld)
            P.drain("vector")
            yv = P.op("vector", "tensor_tensor", sem="vv",
                      out=xr(g, dcx, nt * TT, (nt + 1) * TT),
                      in0=xr(g, dcx, nt * TT, (nt + 1) * TT), in1=yt,
                      op=OP.add)
            st[("yred_used", (dcx * NTT + nt) % 2)] = yv
    g["xres_writer_vv"] = P.snap("vv")


def emit_lat_ln(P, g, ntg, lat_ps, W):
    st = g["_st"]
    scr = g["scr"]
    lraw = [av(g, ASP, W), av(g, ASP + W, W)]
    sq = av(g, ASP + 2 * W, W)
    S2, Q2 = g["p6"], g["p7"]
    for lc in range(LC):
        P.wait("vector", "mm", st[("lat_mm", ntg, lc)])
        P.drain("vector")
        P.op("vector", "tensor_copy", sem="vv", out=lraw[lc],
             in_=lat_ps[lc][:, 0:W])
    lraw_vv = P.snap("vv")
    sqv = {}
    for lc in range(LC):
        P.wait("scalar", "vv", lraw_vv)
        if lc == 1:
            P.wait("scalar", "mm", sqv[0])
        P.drain("scalar")
        P.op("scalar", "activation", sem="ss", out=sq, in_=lraw[lc],
             func=AF.Square)
        sq_ss = P.snap("ss")
        P.wait("tensor", "vv", lraw_vv)
        P.wait("tensor", "ss", sq_ss)
        if lc == 0:
            P.wait("tensor", "vv", st.get("latln_prev_vv", 0))
        P.op("tensor", "matmul", S2[:, 0:W], g["ones_bfS"][:, :], lraw[lc],
             start=(lc == 0), stop=(lc == LC - 1))
        sqv[lc] = P.op("tensor", "matmul", Q2[:, 0:W], g["ones_bfS"][:, :], sq,
                       start=(lc == 0), stop=(lc == LC - 1), sem="mm")
    smm = P.snap("mm")
    mu2, var, xm, rstd = (scr[:, 0:W], scr[:, 512:512 + W],
                          scr[:, 1024:1024 + W], scr[:, 1536:1536 + W])
    P.wait("scalar", "mm", smm)
    P.drain("scalar")
    P.op("scalar", "activation", sem="ss", out=mu2, in_=S2[:, 0:W],
         func=AF.Square, scale=1.0 / LAT)
    P.waitc("vector", "ss")
    P.wait("vector", "mm", smm)
    P.drain("vector")
    P.op("vector", "scalar_tensor_tensor", sem="vv", out=var, in0=Q2[:, 0:W],
         scalar=1.0 / LAT, in1=mu2, op0=OP.mult, op1=OP.subtract)
    P.waitc("scalar", "vv")
    P.drain("scalar")
    P.op("scalar", "activation", sem="ss", out=rstd, in_=var, func=AF.Ln,
         bias=g["ccolS"][:, 0:1], scale=1.0)
    P.drain("scalar")
    P.op("scalar", "activation", sem="ss", out=rstd, in_=rstd, func=AF.Exp,
         scale=-0.5)
    P.waitc("vector", "ss")
    for lc in range(LC):
        P.drain("vector")
        P.op("vector", "scalar_tensor_tensor", out=xm, in0=S2[:, 0:W],
             scalar=-1.0 / LAT, in1=lraw[lc], op0=OP.mult, op1=OP.add)
        P.drain("vector")
        P.op("vector", "tensor_tensor", out=xm, in0=xm, in1=rstd, op=OP.mult)
        P.drain("vector")
        ev = P.op("vector", "tensor_scalar", sem="vv",
                  out=av(g, ALAT + lc * T + ntg * W, W), in0=xm,
                  scalar1=g["lng_sb"][:, lc:lc + 1],
                  scalar2=g["lnb_sb"][:, lc:lc + 1], op0=OP.mult, op1=OP.add)
        st[("lat_evac", ntg, lc)] = ev
    st["latln_prev_vv"] = P.snap("vv")


def emit_rope(P, g, src_ps, mv, dst_bf, tcol0, W):
    scr = g["scr"]
    qsw = scr[:, 0:W]
    t1 = scr[:, 512:512 + W]
    mask = [(i ^ 1) for i in range(32)]
    P.wait("vector", "mm", mv)
    P.drain("vector")
    P.op("vector", "stream_shuffle", out=qsw, in_=src_ps[:, 0:W], mask=mask)
    P.op("vector", "tensor_tensor", out=t1, in0=src_ps[:, 0:W],
         in1=g["cos2S"][:, tcol0:tcol0 + W], op=OP.mult)
    P.drain("vector")
    P.op("vector", "tensor_tensor", out=qsw, in0=qsw,
         in1=g["sin2S"][:, tcol0:tcol0 + W], op=OP.mult)
    P.drain("vector")
    return P.op("vector", "tensor_tensor", sem="vv", out=dst_bf, in0=t1,
                in1=qsw, op=OP.add)


def emit_norm_acc(P, g, which, ntg, acc, abase, col0, W):
    st = g["_st"]
    smf = g["smf"]
    sqs = av(g, ASP, W)
    N_ps = g["p6"]
    mv = None
    for qc in range(2):
        P.wait("scalar", "vv", st[(which + "_done", ntg, qc)])
        P.wait("scalar", "mm", st.get(("nq_mm", which), 0))
        P.drain("scalar")
        P.op("scalar", "activation", sem="ss", out=sqs,
             in_=av(g, abase + qc * T + col0, W), func=AF.Square)
        sq_ss = P.snap("ss")
        P.wait("tensor", "ss", sq_ss)
        if qc == 0:
            P.wait("tensor", "vv", st.get(("nred", which, ntg - 1), 0))
        mv = P.op("tensor", "matmul", N_ps[0:4, 0:W],
                  g["headonesS"][:, qc * 4:(qc + 1) * 4], sqs,
                  start=(qc == 0), stop=(qc == 1), sem="mm")
        st[("nq_mm", which)] = mv
    ntmax = smf[0:4, 20:21]
    P.wait("vector", "mm", mv)
    P.drain("vector")
    P.op("vector", "reduce_max", out=ntmax, in_=N_ps[0:4, 0:W], axis=AXX)
    P.drain("vector")
    if ntg == 0:
        rv = P.op("vector", "tensor_copy", sem="vv", out=acc, in_=ntmax)
    else:
        rv = P.op("vector", "tensor_tensor", sem="vv", out=acc, in0=acc,
                  in1=ntmax, op=OP.max)
    st[("nred", which, ntg)] = rv


def emit_bias(P, g, qmax, kmax):
    st = g["_st"]
    smf = g["smf"]
    t = smf[0:4, 21:22]
    P.drain("vector")
    tv = P.op("vector", "tensor_tensor", sem="vv", out=t, in0=qmax, in1=kmax,
              op=OP.mult)
    P.wait("scalar", "vv", tv)
    P.drain("scalar")
    sv = P.op("scalar", "activation", sem="ss", out=t, in_=t, func=AF.Sqrt)
    P.wait("vector", "ss", sv)
    P.drain("vector")
    bv = P.op("vector", "tensor_scalar", sem="vv", out=t, in0=t,
              scalar1=-0.125, scalar2=0.25, op0=OP.mult, op1=OP.subtract)
    P.wait("tensor", "vv", bv)
    P.wait("tensor", "vv", st["v_done"])
    tr = P.op("tensor", "transpose", g["p7"][0:1, 0:4], t,
              g["identS"][0:4, 0:4], sem="mm")
    P.wait("vector", "mm", tr)
    P.drain("vector")
    rv = P.op("vector", "tensor_copy", sem="vv", out=smf[0:1, 24:28],
              in_=g["p7"][0:1, 0:4])
    P.wait("tensor", "vv", rv)
    bb = P.op("tensor", "matmul", g["p7"][:, 8:12], g["onesrow_fS"][:, :],
              smf[0:1, 24:28], start=True, stop=True, sem="mm")
    P.wait("vector", "mm", bb)
    P.drain("vector")
    P.op("vector", "tensor_copy", sem="vv", out=smf[:, 16:20],
         in_=g["p7"][:, 8:12])
    st["bias_vv"] = P.snap("vv")
    P.wait("scalar", "vv", st["bias_vv"])


def emit_attn_pass(P, g, p):
    st = g["_st"]
    scr, smf = g["scr"], g["smf"]
    wsl = av(g, AWS, 4096)
    qmax = smf[0:4, 8:9]
    kmax = smf[0:4, 9:10]

    P.wait("gpsimd", "mm", st["wsl_free_mm"])
    src = bass.AP(g["dr"]["wq_sh"], 0,
                  [[HL * DH, 128], [128 * HL * DH, DC], [1, HL * DH]])
    P.dma("gpsimd", wsl.rearrange("p (a b) -> p a b", a=DC), src, "gwa")
    wq_gld = P.snap("gwa")

    q_ps = [g["p4"], g["p5"]]
    psum_fence(P)
    st[("xti_used", 0)] = max(st.get(("xti_used", 0), 0), st["wsl_free_mm"])
    st[("xti_used", 1)] = max(st.get(("xti_used", 1), 0), st["wsl_free_mm"])
    W2 = 512
    for ntg in range(NQG):
        half = ntg // 4
        xti = av(g, AO + (ntg % 2) * 4096, 4096)
        xsem = "xta" if ntg % 2 == 0 else "xtb"
        P.wait("sync", "cc", g["ag_cc"])
        P.wait("sync", "mm", st.get(("xti_used", ntg % 2), 0))
        src = bass.AP(g["xt_gath"], half * D * NT + (ntg % 4) * W2,
                      [[NT, 128], [128 * NT, DC], [1, W2]])
        P.dma("sync", xti.rearrange("p (c j) -> p c j", c=DC), src, xsem)
        xt_sld = P.snap(xsem)
        for qc in range(2):
            P.wait("tensor", xsem, xt_sld)
            P.wait("tensor", "gwa", wq_gld)
            P.wait("tensor", "vv", st.get(("q_done", ntg - 1, qc), 0))
            mv = None
            for c in range(DC):
                mv = P.op("tensor", "matmul", q_ps[qc][:, 0:W2],
                          wsl[:, c * 512 + (p * 2 + qc) * 128:
                              c * 512 + (p * 2 + qc) * 128 + 128],
                          xti[:, c * W2:(c + 1) * W2],
                          start=(c == 0), stop=(c == DC - 1),
                          sem="mm" if c == DC - 1 else None)
            st[("q_done", ntg, qc)] = emit_rope(
                P, g, q_ps[qc], mv, av(g, AQ + qc * T + ntg * W2, W2),
                ntg * W2, W2)
        st[("xti_used", ntg % 2)] = P.snap("mm")
        emit_norm_acc(P, g, "q", ntg, qmax, AQ, ntg * W2, W2)
    q_all_mm = P.snap("mm")

    P.wait("gpsimd", "mm", q_all_mm)
    src = bass.AP(g["dr"]["wuk_sh"], 0,
                  [[HL * DH, 128], [128 * HL * DH, LC], [1, HL * DH]])
    P.dma("gpsimd", wsl[:, 0:LC * HL * DH].rearrange("p (a b) -> p a b", a=LC),
          src, "gwa")
    wuk_gld = P.snap("gwa")
    for ntg in range(NQG):
        for kc in range(2):
            P.wait("tensor", "gwa", wuk_gld)
            P.wait("tensor", "vv", g["lat_vv"])
            if ntg == 0:
                P.wait("tensor", "vv", st[("q_done", NQG - 1, kc)])
            P.wait("tensor", "vv", st.get(("k_done", ntg - 1, kc), 0))
            mv = None
            for lc in range(LC):
                mv = P.op("tensor", "matmul", q_ps[kc][:, :],
                          wsl[:, lc * 512 + (p * 2 + kc) * 128:
                              lc * 512 + (p * 2 + kc) * 128 + 128],
                          av(g, ALAT + lc * T + ntg * TT, TT),
                          start=(lc == 0), stop=(lc == LC - 1),
                          sem="mm" if lc == LC - 1 else None)
            st[("k_done", ntg, kc)] = emit_rope(
                P, g, q_ps[kc], mv, av(g, AK + kc * T + ntg * TT, TT),
                ntg * TT, TT)
        emit_norm_acc(P, g, "k", ntg, kmax, AK, ntg * TT, TT)
    k_all_mm = P.snap("mm")

    P.wait("gpsimd", "mm", k_all_mm)
    src = bass.AP(g["dr"]["wuv_sh"], 0,
                  [[HL * DH, 128], [128 * HL * DH, LC], [1, HL * DH]])
    P.dma("gpsimd", wsl[:, 0:LC * HL * DH].rearrange("p (a b) -> p a b", a=LC),
          src, "gwa")
    wuv_gld = P.snap("gwa")
    P.wait("vector", "vv", st.get("v_used", 0))
    P.drain("vector")
    ones_vv = P.op("vector", "memset",
                   av(g, AV, 32 * 260).rearrange("p (kt h j) -> p kt h j",
                                                 kt=32, h=4)[:, :, :, 64:65],
                   1.0, sem="vv")
    vb2 = [g["p7"], g["p6"]]
    psum_fence(P)
    for kt in range(NKT):
        vb = vb2[kt % 2]
        P.wait("tensor", "gwa", wuv_gld)
        P.wait("tensor", "vv", g["lat_vv"])
        if kt < 2:
            P.wait("tensor", "vv", st.get(("nred", "k", NQG - 1), 0))
        P.wait("tensor", "vv", st.get(("v_evac", kt - 2), 0))
        mv = None
        for lc in range(LC):
            mv = P.op("tensor", "matmul", vb[:, 0:256],
                      av(g, ALAT + lc * T + kt * 128, 128),
                      wsl[:, lc * 512 + p * 256: lc * 512 + (p + 1) * 256],
                      start=(lc == 0), stop=(lc == LC - 1),
                      sem="mm" if lc == LC - 1 else None)
        P.wait("vector", "mm", mv)
        P.wait("vector", "vv", ones_vv)
        P.drain("vector")
        st[("v_evac", kt)] = P.op(
            "vector", "tensor_copy", sem="vv",
            out=av(g, AV + kt * 260, 260).rearrange("p (h j) -> p h j",
                                                    h=4)[:, :, 0:64],
            in_=vb[:, 0:256].rearrange("p (h j) -> p h j", h=4))
    st["v_done"] = P.snap("vv")

    emit_bias(P, g, qmax, kmax)

    P.waitc("scalar", "bna")
    P.waitc("scalar", "bnb")
    P.waitc("scalar", "ypa")
    P.waitc("scalar", "ypb")
    sgrp = [g["pbig0"], g["pbig1"], g["pbig2"]]
    obA, obB = g["p6"], g["p7"]
    psum_fence(P)
    if not st.get("lrinit"):
        st["lrinit"] = True
        P.drain("vector")
        P.op("vector", "memset", smf[0:33, 512:1024], 0.0)
    for pr in range(2):
        P.drain("vector")
        P.op("vector", "tensor_tensor", sem="vv", out=smf[:, 22 + pr:23 + pr],
             in0=smf[:, 16 + 2 * pr:17 + 2 * pr],
             in1=smf[:, 17 + 2 * pr:18 + 2 * pr], op=OP.min)
    pb_vv = P.snap("vv")
    P.wait("scalar", "vv", pb_vv)
    def norm_part2(P, g, pend):
        prq, qgq, lrss = pend
        P.wait("tensor", "ss", lrss)
        P.wait("tensor", "sl2", g["sel2_v"])
        P.wait("tensor", "vv", st.get("lbev_vv", 0))
        lb = P.op("tensor", "matmul", g["pbig2"][:, 0:512],
                  avr(g, ASP + 1536, 256, 0, 33).bitcast(F32),
                  smf[0:33, 512:1024], start=True, stop=True, sem="mm")
        lbev = scr[:, 1024:1536]
        P.wait("vector", "mm", lb)
        P.drain("vector")
        P.op("vector", "tensor_copy", sem="vv", out=lbev,
             in_=g["pbig2"][:, 0:512])
        P.drain("vector")
        P.op("vector", "tensor_tensor", sem="vv",
             out=avr(g, AO + prq * T + qgq * TT, TT, 0, 64),
             in0=scr[0:64, 0:512], in1=lbev[0:64, :], op=OP.mult)
        P.op("vector", "tensor_tensor", sem="vv",
             out=avr(g, AO + prq * T + qgq * TT, TT, 64, 128),
             in0=scr[64:128, 512:1024], in1=lbev[64:128, :], op=OP.mult)
        st["lbev_vv"] = P.snap("vv")

    for pr in range(2):
        hA, hB = 2 * pr, 2 * pr + 1
        for qg in range(NQG):
            nkt = 4 * (qg + 1)
            for step in range(nkt + 2):
                if step >= 2:
                    kt = step - 2
                    m0 = max(0, kt - 4 * qg)
                    ptv = av(g, APT + (kt % 2) * 1024, 1024)
                    rsem, rval = st[("pt_ready", kt)]
                    P.wait("tensor", rsem, rval)
                    if kt == 0:
                        P.wait("tensor", "vv", st.get("o_free", 0))
                        P.wait("tensor", "vv", st.get("lbev_vv", 0))
                    P.op("tensor", "matmul", obA[0:65, m0 * 128:512],
                         av(g, AV + kt * 260 + hA * 65, 65),
                         ptv[:, m0 * 128:512],
                         start=(kt == 0), stop=(kt == nkt - 1))
                    avmm = P.op("tensor", "matmul", obB[0:65, m0 * 128:512],
                                av(g, AV + kt * 260 + hB * 65, 65),
                                ptv[:, 512 + m0 * 128:1024],
                                start=(kt == 0), stop=(kt == nkt - 1), sem="mm")
                    st[("pt_av", kt % 2)] = avmm
                if step == 2 and st.get("pending_norm"):
                    norm_part2(P, g, st.pop("pending_norm"))
                if step < nkt:
                    kt = step
                    gi = kt % 3
                    grp = sgrp[gi]
                    m0 = max(0, kt - 4 * qg)
                    P.wait("tensor", "ss", st.get(("exp_ss", gi), 0))
                    if kt == 0:
                        P.wait("tensor", "vv", st["v_done"])
                    if gi == 2:
                        P.wait("tensor", "vv", st.get("lbev_vv", 0))
                    P.op("tensor", "matmul", grp[:, 0:512],
                         avr(g, AK + pr * T + kt * 128, 128, 0, 64),
                         avr(g, AQ + pr * T + qg * TT, TT, 0, 64),
                         start=True, stop=True, tile_position=(0, 0))
                    smm = P.op("tensor", "matmul", grp[:, 512:1024],
                               avr(g, AK + pr * T + kt * 128, 128, 64, 128),
                               avr(g, AQ + pr * T + qg * TT, TT, 64, 128),
                               start=True, stop=True, sem="mm",
                               tile_position=(64, 0))
                    ptv = av(g, APT + (kt % 2) * 1024, 1024)
                    P.wait("scalar", "mm", smm)
                    P.wait("scalar", "mm", st.get(("pt_av", kt % 2), 0))
                    P.wait("scalar", "vv", st.get(("pt_msk", kt % 2), 0))
                    P.drain("scalar")
                    gv = grp[:, :].rearrange("p (h q) -> p h q", h=2)
                    pv = ptv.rearrange("p (h q) -> p h q", h=2)
                    es_ = P.op("scalar", "activation", sem="ss",
                               out=pv[:, :, m0 * 128:512],
                               in_=gv[:, :, m0 * 128:512], func=AF.Exp,
                               bias=smf[:, 22 + pr:23 + pr], scale=0.125)
                    st[("exp_ss", gi)] = es_
                    if kt >= 4 * qg:
                        P.wait("vector", "ss", es_)
                        P.drain("vector")
                        P.op("vector", "tensor_tensor",
                             out=ptv[:, m0 * 128:512],
                             in0=ptv[:, m0 * 128:512],
                             in1=g["masksS"][:, m0 * 512 + m0 * 128:
                                             (m0 + 1) * 512],
                             op=OP.mult)
                        mk = P.op("vector", "tensor_tensor", sem="vv",
                                  out=ptv[:, 512 + m0 * 128:1024],
                                  in0=ptv[:, 512 + m0 * 128:1024],
                                  in1=g["masksS"][:, m0 * 512 + m0 * 128:
                                                  (m0 + 1) * 512],
                                  op=OP.mult)
                        st[("pt_msk", kt % 2)] = mk
                        st[("pt_ready", kt)] = ("vv", mk)
                    else:
                        st[("pt_ready", kt)] = ("ss", es_)
            av_mm = P.snap("mm")
            P.wait("vector", "mm", av_mm)
            P.wait("vector", "ss", st.get("lr_ss", 0))
            P.drain("vector")
            c1 = P.op("vector", "tensor_copy", sem="vv", out=scr[0:65, 0:512],
                      in_=obA[0:65, :])
            P.drain("vector")
            c2 = P.op("vector", "tensor_copy", sem="vv",
                      out=scr[32:33, 1536:2048], in_=obB[64:65, :])
            P.drain("vector")
            ofree = P.op("vector", "tensor_copy", sem="vv",
                         out=scr[64:128, 512:1024], in_=obB[0:64, :])
            st["o_free"] = ofree
            P.wait("scalar", "vv", c1)
            P.drain("scalar")
            P.op("scalar", "activation", out=smf[0:1, 512:1024],
                 in_=scr[64:65, 0:512], func=AF.Ln)
            P.wait("scalar", "vv", c2)
            P.op("scalar", "activation", out=smf[32:33, 512:1024],
                 in_=scr[32:33, 1536:2048], func=AF.Ln)
            P.drain("scalar")
            P.op("scalar", "activation", out=smf[0:1, 512:1024],
                 in_=smf[0:1, 512:1024], func=AF.Exp, scale=-1.0)
            lr_ss = P.op("scalar", "activation", sem="ss",
                         out=smf[32:33, 512:1024], in_=smf[32:33, 512:1024],
                         func=AF.Exp, scale=-1.0)
            st["lr_ss"] = lr_ss
            st["pending_norm"] = (pr, qg, lr_ss)
    if st.get("pending_norm"):
        norm_part2(P, g, st.pop("pending_norm"))
    o_done = P.snap("vv")
    st["v_used"] = o_done

    P.wait("gpsimd", "vv", o_done)
    wo = av(g, AV, 2048)
    src = bass.AP(g["dr"]["wo_sh"], p * 2 * 128 * D,
                  [[D, 128], [128 * D, 2], [1, D]])
    P.dma("gpsimd", wo.rearrange("p (a b) -> p a b", a=2), src, "gwa")
    wo_gld = P.snap("gwa")
    ytileA = av(g, APT, 512)
    ytileB = av(g, APT + 512, 512)
    yprev = av(g, APT + 1024, 512)
    yprev2 = av(g, APT + 1536, 512)
    P.wait("sync", "mm", P.snap("mm"))
    P.wait("sync", "vv", P.snap("vv"))
    psum_fence(P)
    yb3 = [g["p4"], g["p5"]]
    for ntg in range(NQG):
        for dcx in range(DC):
            i2 = (ntg * DC + dcx) % 2
            yb2 = yb3[i2]
            ytile = ytileA if i2 == 0 else ytileB
            P.wait("tensor", "gwa", wo_gld)
            P.wait("tensor", "vv", o_done)
            P.wait("tensor", "vv", st.get(("yt_used", i2), 0))
            mv = None
            for oc in range(2):
                mv = P.op("tensor", "matmul", yb2[:, :],
                          wo[:, oc * 1024 + dcx * 128: oc * 1024 + (dcx + 1) * 128],
                          av(g, AO + oc * T + ntg * TT, TT),
                          start=(oc == 0), stop=(oc == 1),
                          sem="mm" if oc == 1 else None)
            bsem = "bna" if i2 == 0 else "bnb"
            if p == 0:
                P.wait("vector", "mm", mv)
                P.wait("vector", bsem, st.get(("yt_sst", i2), 0))
                P.drain("vector")
                ytv = P.op("vector", "tensor_copy", sem="vv", out=ytile,
                           in_=yb2[:, :])
                st[("yt_used", i2)] = ytv
                P.wait("sync", "vv", ytv)
                dst = bass.AP(g["yacc"], dcx * 128 * T + ntg * TT,
                              [[T, 128], [1, TT]])
                P.dma("sync", dst, ytile, bsem)
                st[("yt_sst", i2)] = P.snap(bsem)
            else:
                ypsem = "ypa" if i2 == 0 else "ypb"
                P.wait("sync", "vv", st.get(("yp_used", i2), 0))
                src2 = bass.AP(g["yacc"], dcx * 128 * T + ntg * TT,
                               [[T, 128], [1, TT]])
                P.dma("sync", yprev if i2 == 0 else yprev2, src2, ypsem)
                ypl = P.snap(ypsem)
                P.wait("vector", "mm", mv)
                P.wait("vector", ypsem, ypl)
                P.wait("vector", bsem, st.get(("yt_sst", i2), 0))
                P.drain("vector")
                ytv = P.op("vector", "tensor_tensor", sem="vv", out=ytile,
                           in0=yb2[:, :], in1=yprev if i2 == 0 else yprev2,
                           op=OP.add)
                st[("yt_used", i2)] = ytv
                st[("yp_used", i2)] = ytv
                P.wait("sync", "vv", ytv)
                half = ntg // 4
                dst = bass.AP(g["y_bounce"],
                              (half * D + dcx * 128) * NT + (ntg % 4) * TT,
                              [[NT, 128], [1, TT]])
                P.dma("sync", dst, ytile, bsem)
                st[("yt_sst", i2)] = P.snap(bsem)
    st["wsl_free_mm"] = P.snap("mm")


def emit_output(P, g):
    if "dbg_gates" in g and g["stage"] == "ffn1":
        P.waitc("sync", "vv")
        P.dma("sync", g["dbg_gates"][:, :], g["gates_fm"][0:8, :], "outw")
        P.dma("sync", g["dbg_xt"][:, :], av(g, AXT, 2048), "outw")
        P.dma("sync", g["dbg_h"][:, :], av(g, AH, 2048), "outw")
    P.waitc("sync", "vv")
    for c in range(DC):
        dst = bass.AP(g["out"], c * 128 * NT, [[NT, 128], [1, NT]])
        P.dma("sync", dst, g["x_res"][:, c * NT:(c + 1) * NT], "outw")
    P.waitc("sync", "outw")



def rope_perm():
    perm = np.zeros(D, dtype=np.int64)
    for h in range(H):
        for j in range(DH // 2):
            perm[h * DH + 2 * j] = h * DH + j
            perm[h * DH + 2 * j + 1] = h * DH + DH // 2 + j
    return perm


def host_constants():
    c = {}
    c["ident"] = np.eye(128, dtype=np.float32)
    c["ones_bf"] = np.ones((128, 128), dtype=BF)
    c["onesrow_f"] = np.ones((1, 128), dtype=np.float32)
    ho = np.zeros((128, 8), dtype=BF)
    for qc in range(2):
        for hh in range(2):
            ho[hh * 64:(hh + 1) * 64, qc * 4 + 2 * qc + hh] = 1.0
    ho = np.zeros((128, 8), dtype=BF)
    for qc in range(2):
        for hh in range(2):
            m = 2 * qc + hh
            ho[hh * 64:(hh + 1) * 64, qc * 4 + m] = 1.0
    c["headones"] = ho
    half = DH // 2
    pos = np.arange(T, dtype=np.float64)
    inv = 1.0 / (10000.0 ** (np.arange(half, dtype=np.float64) / half))
    ang = pos[None, :] * inv[:, None]
    cosr = np.cos(ang)
    sinr = np.sin(ang)
    cos2 = np.zeros((128, T), dtype=np.float32)
    sin2 = np.zeros((128, T), dtype=np.float32)
    for r in range(128):
        j = (r % 64) // 2
        cos2[r] = cosr[j]
        sin2[r] = sinr[j] if (r % 2 == 1) else -sinr[j]
    c["cos2"] = cos2.astype(BF)
    c["sin2"] = sin2.astype(BF)
    masks = np.zeros((128, 4 * TT), dtype=BF)
    for m in range(4):
        kpos = m * 128 + np.arange(128)[:, None]
        qpos = np.arange(TT)[None, :]
        masks[:, m * TT:(m + 1) * TT] = (kpos <= qpos).astype(BF)
    c["masks"] = masks
    cc = np.zeros((128, 4), dtype=np.float32)
    cc[:, 0] = EPS
    cc[:, 1] = 1.0
    c["ccol"] = cc
    sel = np.zeros((8, E * 128), dtype=BF)
    for e in range(E):
        sel[e, e * 128:(e + 1) * 128] = 1.0
    c["sel8"] = sel
    s2 = np.zeros((33, 128), dtype=np.float32)
    s2[0, 0:64] = 1.0
    s2[32, 64:128] = 1.0
    c["sel2"] = s2
    return c


_NC_CACHE = {}


def kernel(L0_attn_x, L0_attn_W_q, L0_attn_W_dkv, L0_attn_W_uk, L0_attn_W_uv,
           L0_attn_W_o, L0_attn_ln_gamma, L0_attn_ln_beta, L0_ffn_gate_w,
           L0_ffn_gate_bias, L0_ffn_w1, L0_ffn_w3, L0_ffn_w2, L0_ffn_sw1,
           L0_ffn_sw3, L0_ffn_sw2, stage="full"):
    perm = rope_perm()
    wq_p = np.ascontiguousarray(np.asarray(L0_attn_W_q)[:, perm])
    wuk_p = np.ascontiguousarray(np.asarray(L0_attn_W_uk)[:, perm])
    con = host_constants()

    base = dict(con)
    base["w1t"] = np.ascontiguousarray(np.asarray(L0_ffn_w1).transpose(0, 2, 1))
    base["w3t"] = np.ascontiguousarray(np.asarray(L0_ffn_w3).transpose(0, 2, 1))
    base["w2t"] = np.ascontiguousarray(np.asarray(L0_ffn_w2).transpose(0, 2, 1))
    base["sw1t"] = np.ascontiguousarray(np.asarray(L0_ffn_sw1).T)
    base["sw3t"] = np.ascontiguousarray(np.asarray(L0_ffn_sw3).T)
    base["sw2t"] = np.ascontiguousarray(np.asarray(L0_ffn_sw2).T)
    base["gwt"] = np.ascontiguousarray(np.asarray(L0_ffn_gate_w).T)
    base["gbias"] = np.asarray(L0_ffn_gate_bias).reshape(1, E).astype(np.float32)
    base["wdkv"] = np.ascontiguousarray(np.asarray(L0_attn_W_dkv))
    lg = np.asarray(L0_attn_ln_gamma).reshape(LC, 128).T
    lb = np.asarray(L0_attn_ln_beta).reshape(LC, 128).T
    base["ln_g"] = np.ascontiguousarray(lg.astype(np.float32))
    base["ln_b"] = np.ascontiguousarray(lb.astype(np.float32))

    x = np.asarray(L0_attn_x)
    x_fm = x.transpose(0, 2, 1)

    in_maps = []
    for core in range(8):
        b, hh = core // 2, core % 2
        m = dict(base)
        m["x_sh"] = np.ascontiguousarray(x_fm[b, :, hh * NT:(hh + 1) * NT])
        hc = slice(hh * HL * DH, (hh + 1) * HL * DH)
        m["wq_sh"] = np.ascontiguousarray(wq_p[:, hc])
        m["wuk_sh"] = np.ascontiguousarray(wuk_p[:, hc])
        m["wuv_sh"] = np.ascontiguousarray(np.asarray(L0_attn_W_uv)[:, hc])
        m["wo_sh"] = np.ascontiguousarray(np.asarray(L0_attn_W_o)[hc, :])
        in_maps.append(m)

    key = stage
    if key not in _NC_CACHE:
        _NC_CACHE[key] = build_program(stage)
    nc = _NC_CACHE[key]
    res = run_bass_kernel_spmd(nc, in_maps, core_ids=list(range(8)))
    outs = []
    for b in range(BATCH):
        lo = res.results[2 * b]["out"]
        hi = res.results[2 * b + 1]["out"]
        outs.append(np.concatenate([lo, hi], axis=1).T)
    return np.stack(outs).astype(np.float32)
